# version 06
# baseline (speedup 1.0000x reference)
import contextlib
import numpy as np
import ml_dtypes
import concourse.bass as bass
import concourse.mybir as mybir
from concourse.bass_utils import run_bass_kernel_spmd
from concourse.bass_core_rust import IndirectOffsetOnAxis

F32 = mybir.dt.float32
BF16 = mybir.dt.bfloat16
I32 = mybir.dt.int32
AF = mybir.ActivationFunctionType
ALU = mybir.AluOpType

D = 1024
NF = 8
HEADS = 4
C = 512
TS = 512
HID = 3072
HC = 512
NHC = HID // HC
EPS = 1e-5
SCALE_O = float(128) ** -0.5
Q127 = 1.0 / 127.0
Q127SQ = 1.0 / (127.0 * 127.0)


class Ctr:
    def __init__(self, sem):
        self.sem = sem
        self.n = 0

    def inc(self, inst, k=1):
        inst.then_inc(self.sem, k)
        self.n += k
        return self.n


def build_program(T_full, T_own, dbg=False):
    NCH = T_full // C
    NTS = T_own // TS
    NROW = NCH * HEADS * 128

    nc = bass.Bass("TRN2", target_bir_lowering=False)

    xT = nc.dram_tensor("xT", [D, T_full], F32, kind="ExternalInput").ap()
    xoT = nc.dram_tensor("xoT", [D, T_own], F32, kind="ExternalInput").ap()
    wqT = nc.dram_tensor("wqT", [D, 512], F32, kind="ExternalInput").ap()
    wkT = nc.dram_tensor("wkT", [D, 512], F32, kind="ExternalInput").ap()
    wvT = nc.dram_tensor("wvT", [D, 512], F32, kind="ExternalInput").ap()
    woT = nc.dram_tensor("woT", [D, D], F32, kind="ExternalInput").ap()
    wg1T = nc.dram_tensor("wg1T", [D, 128], F32, kind="ExternalInput").ap()
    wg2T = nc.dram_tensor("wg2T", [128, D], F32, kind="ExternalInput").ap()
    gnorm = nc.dram_tensor("gnorm", [128, NF], F32, kind="ExternalInput").ap()
    wgTd = nc.dram_tensor("wgTd", [D, HID], I32, kind="ExternalInput").ap()
    wuTd = nc.dram_tensor("wuTd", [D, HID], I32, kind="ExternalInput").ap()
    wdTd = nc.dram_tensor("wdTd", [HID, D], I32, kind="ExternalInput").ap()
    umask = nc.dram_tensor("umask", [C, C], F32, kind="ExternalInput").ap()
    oidx = nc.dram_tensor("oidx", [128, NTS * 8], I32, kind="ExternalInput").ap()
    ridx = nc.dram_tensor("ridx", [128, NTS], I32, kind="ExternalInput").ap()
    out = nc.dram_tensor("out", [D, T_own], F32, kind="ExternalOutput").ap()

    o_dram = nc.dram_tensor("o_dram", [NROW, 512], BF16, kind="Internal")
    cc_out = nc.dram_tensor("cc_out", [2 * NROW, 512], BF16, kind="Internal")
    wgbf = nc.dram_tensor("wgbf", [NHC * 128, NF * HC], BF16, kind="Internal").ap()
    wubf = nc.dram_tensor("wubf", [NHC * 128, NF * HC], BF16, kind="Internal").ap()
    rows_dram = nc.dram_tensor("rows_dram", [NCH, 2 * C], BF16, kind="Internal")
    groups = [[0, 1], [2, 3], [4, 5], [6, 7]]

    es = contextlib.ExitStack()
    with es:
        sb = lambda *a: es.enter_context(nc.sbuf_tensor(*a))
        ps = lambda *a: es.enter_context(nc.psum_tensor(*a))
        sem = lambda n: es.enter_context(nc.semaphore(n))
        wd_sb = sb("wd_sb", [128, (HID // 128) * D], BF16)
        wo_sb = sb("wo_sb", [128, NF * D], BF16)
        wg1_sb = sb("wg1_sb", [128, NF * 128], BF16)
        wg2_sb = sb("wg2_sb", [128, D], BF16)
        gn_sb = sb("gn_sb", [128, NF], F32)
        oidx_sb = sb("oidx_sb", [128, NTS * 8], I32)
        ridx_sb = sb("ridx_sb", [128, NTS], I32)
        ones_k1 = sb("ones_k1", [1, 128], BF16)
        ones_m1 = sb("ones_m1", [128, 1], BF16)
        rows_bf = sb("rows_bf", [1, 2 * C], BF16)
        eps_sb = sb("eps_sb", [1, 1], F32)
        s_x = sem("s_x"); s_o = sem("s_o"); s_cc = sem("s_cc")
        s_x2 = sem("s_x2"); s_wg = sem("s_wg"); s_og = sem("s_og")
        s_st = sem("s_st"); s_ld = sem("s_ld")
        s_wu = sem("s_wu"); s_pcr = sem("s_pcr"); s_pcw = sem("s_pcw")
        s_lnr = sem("s_lnr"); s_lr = sem("s_lr")
        _pe = sem("c_pe"); _sc = sem("c_sc"); _ve = sem("c_ve")
        s_p1 = sem("s_p1")
        cpe, csc, cve = Ctr(_pe), Ctr(_sc), Ctr(_ve)
        M = {}
        nsetup = [0]

        with nc.Block() as b0:
            def g_setup(g):
                loads = [
                    (wd_sb[:].rearrange("p (kt d) -> p kt d", d=D),
                     wdTd.rearrange("(kt p) d -> p kt d", p=128)),
                    (wo_sb[:].rearrange("p (f d) -> p f d", d=D),
                     woT.rearrange("(f p) d -> p f d", p=128)),
                    (wg1_sb[:].rearrange("p (f l) -> p f l", l=128),
                     wg1T.rearrange("(f p) l -> p f l", p=128)),
                    (wg2_sb[:], wg2T),
                    (gn_sb[:], gnorm),
                    (oidx_sb[:], oidx),
                    (ridx_sb[:], ridx),
                ]
                for dst, src in loads:
                    g.dma_start(out=dst, in_=src).then_inc(s_ld, 16)
                    nsetup[0] += 1
                g.memset(eps_sb[:], EPS)
                g.memset(ones_k1[:], 1.0)
                g.memset(ones_m1[:], 1.0)
            b0.gpsimd(g_setup)

            def v_setup(v):
                v.wait_ge(s_ld, 16 * nsetup[0])
                ins = None
                for f in range(NF):
                    ins = v.tensor_scalar(
                        wo_sb[:, f * D:(f + 1) * D], wo_sb[:, f * D:(f + 1) * D],
                        gn_sb[:, f:f + 1], 0.5, ALU.mult, ALU.mult)
                M["wo_scaled"] = cve.inc(ins)
            b0.vector(v_setup)

        es1 = contextlib.ExitStack()
        with es1:
            sb1 = lambda *a: es1.enter_context(nc.sbuf_tensor(*a))
            ps1 = lambda *a: es1.enter_context(nc.psum_tensor(*a))
            wq_sb = sb1("wq_sb", [128, NF * 512], BF16)
            wk_sb = sb1("wk_sb", [128, NF * 512], BF16)
            wv_sb = sb1("wv_sb", [128, NF * 512], BF16)
            x_sb = sb1("x_sb", [128, 2 * NF * C], F32)
            xbf_sb = sb1("xbf_sb", [128, NF * C], BF16)
            xn_sb = sb1("xn_sb", [128, NF * C], BF16)
            efm_sb = sb1("efm_sb", [128, HEADS * C], BF16)
            etm_sb = sb1("etm_sb", [128, 4 * 512], BF16)
            vtm_sb = sb1("vtm_sb", [128, 4 * 512], BF16)
            qh_sb = sb1("qh_sb", [128, HEADS * C], BF16)
            ssb_sb = sb1("ssb_sb", [128, 4 * C], BF16)
            tq_sb = sb1("tq_sb", [128, C], F32)
            m1_sb = sb1("m1_sb", [128, C], F32)
            rr_sb = sb1("rr_sb", [128, C], F32)
            zf_sb = sb1("zf_sb", [128, C], F32)
            zcar_sb = sb1("zcar_sb", [128, HEADS], F32)
            abf_sb = sb1("abf_sb", [128, HEADS * 128], BF16)
            osb_sb = sb1("osb_sb", [128, 2 * 512], BF16)
            cast_sb = sb1("cast_sb", [128, 2 * NF * HC], BF16)
            u_sb = sb1("u_sb", [128, (C // 128) * C], BF16)
            rows_sb = sb1("rows_sb", [1, 4 * C], F32)
            pA = ps1("pA", [128, 512], F32)
            pB1 = ps1("pB1", [128, C], F32)
            pB2 = ps1("pB2", [128, C], F32)
            pK1 = ps1("pK1", [128, C], F32)
            pK2 = ps1("pK2", [128, C], F32)
            pW1 = ps1("pW1", [128, C], F32)
            pW2 = ps1("pW2", [128, C], F32)
            pW3 = ps1("pW3", [128, C], F32)
            b1 = es1.enter_context(nc.Block())

            def xv(j, f):
                base = (j % 2) * NF * C
                return x_sb[:, base + f * C: base + f * C + C]

            def fsl(buf, f, w=C):
                return buf[:, f * w:(f + 1) * w]

            def g_qkv(g):
                for dst, src in ((wq_sb, wqT), (wk_sb, wkT), (wv_sb, wvT)):
                    g.dma_start(
                        out=dst[:].rearrange("p (f i) -> p f i", i=512),
                        in_=src.rearrange("(f p) i -> p f i", p=128),
                    ).then_inc(s_ld, 16)
                g.dma_start(
                    out=u_sb[:].rearrange("p (s t) -> p s t", t=C),
                    in_=umask.rearrange("(s p) t -> p s t", p=128),
                ).then_inc(s_ld, 16)
                nsetup[0] += 4
            b1.gpsimd(g_qkv)

            def v_init(v):
                v.wait_ge(s_ld, 16 * nsetup[0])
                v.memset(zcar_sb[:], 0.0)
                ins = v.memset(abf_sb[:], 0.0)
                M["p1_init"] = cve.inc(ins)
            b1.vector(v_init)

            o_store_cnt = [0]

            for j in range(NCH):
                def s_load(s, j=j):
                    if j >= 2:
                        s.wait_ge(cve.sem, M[("xfree", j - 2)])
                    s.dma_start(
                        out=x_sb[:, (j % 2) * NF * C:(j % 2 + 1) * NF * C]
                        .rearrange("p (f t) -> p f t", t=C),
                        in_=xT.rearrange("(f p) t -> p f t", p=128)[:, :, j * C:(j + 1) * C],
                    ).then_inc(s_x, 16)
                b1.sync(s_load)

                def v_xbf(v, j=j):
                    v.wait_ge(s_x, 16 * (j + 1))
                    if j >= 1:
                        v.wait_ge(cpe.sem, M[("statsq", j - 1)])
                    ins = None
                    for f in range(NF):
                        ins = v.tensor_copy(fsl(xbf_sb, f), xv(j, f))
                    M[("xbf", j)] = cve.inc(ins)
                b1.vector(v_xbf)

                def t_stat1(t, j=j):
                    t.wait_ge(cve.sem, M[("xbf", j)])
                    if j == 0:
                        t.wait_ge(cve.sem, M["wo_scaled"])
                        t.wait_ge(s_ld, 16 * nsetup[0])
                    else:
                        t.wait_ge(cve.sem, M[("stev", j - 1, HEADS - 1, 3)])
                    ins = None
                    for f in range(NF):
                        ins = t.matmul(pW1[0:1, :], ones_m1[:], fsl(xbf_sb, f),
                                       start=(f == 0), stop=(f == NF - 1))
                    M[("statsum", j)] = cpe.inc(ins)
                b1.tensor(t_stat1)

                def s_sq(s, j=j):
                    s.wait_ge(cpe.sem, M[("statsum", j)])
                    ins = None
                    for f in range(NF):
                        ins = s.activation(fsl(xbf_sb, f), fsl(xbf_sb, f), AF.Square)
                    M[("sq", j)] = csc.inc(ins)
                b1.scalar(s_sq)

                def t_stat2(t, j=j):
                    t.wait_ge(csc.sem, M[("sq", j)])
                    ins = None
                    for f in range(NF):
                        ins = t.matmul(pW2[0:1, :], ones_m1[:], fsl(xbf_sb, f),
                                       start=(f == 0), stop=(f == NF - 1))
                    M[("statsq", j)] = cpe.inc(ins)
                b1.tensor(t_stat2)

                def s_rowA(s, j=j):
                    s.wait_ge(cpe.sem, M[("statsq", j)])
                    s.mul(rows_sb[:, 0:C], pW1[0:1, :], 1.0 / D)
                    ins = s.mul(rows_sb[:, C:2 * C], pW2[0:1, :], 1.0 / D)
                    M[("rowA", j)] = csc.inc(ins)
                b1.scalar(s_rowA)

                def v_rowB(v, j=j):
                    v.wait_ge(csc.sem, M[("rowA", j)])
                    v.tensor_mul(rows_sb[:, 2 * C:3 * C], rows_sb[:, 0:C], rows_sb[:, 0:C])
                    ins = v.tensor_sub(rows_sb[:, 2 * C:3 * C], rows_sb[:, C:2 * C],
                                       rows_sb[:, 2 * C:3 * C])
                    M[("var", j)] = cve.inc(ins)
                b1.vector(v_rowB)

                def s_rowC(s, j=j):
                    s.wait_ge(cve.sem, M[("var", j)])
                    if j >= 1:
                        s.wait_ge(s_lnr, 16 * j)
                    s.activation(rows_sb[:, 3 * C:4 * C], rows_sb[:, 2 * C:3 * C],
                                 AF.Ln, bias=eps_sb[:])
                    ins = s.activation(rows_bf[:, 0:C], rows_sb[:, 3 * C:4 * C],
                                       AF.Exp, scale=-0.5)
                    M[("rstd", j)] = csc.inc(ins)
                b1.scalar(s_rowC)

                def v_rowD(v, j=j):
                    v.wait_ge(csc.sem, M[("rstd", j)])
                    ins = v.tensor_tensor(rows_bf[:, C:2 * C], rows_sb[:, 0:C],
                                          rows_bf[:, 0:C], ALU.mult)
                    M[("mr", j)] = cve.inc(ins)
                b1.vector(v_rowD)

                def g_rows(g, j=j):
                    g.wait_ge(cve.sem, M[("mr", j)])
                    g.dma_start(out=rows_dram[j:j + 1, :],
                                in_=rows_bf[0:1, 0:2 * C]).then_inc(s_lnr, 16)
                b1.gpsimd(g_rows)

                def t_bc(t, j=j):
                    t.wait_ge(cve.sem, M[("mr", j)])
                    if j >= 1:
                        t.wait_ge(csc.sem, M[("oev", j - 1, HEADS - 1)])
                    t.matmul(pB1[:], ones_k1[:], rows_bf[:, 0:C], start=True, stop=True)
                    ins = t.matmul(pB2[:], ones_k1[:], rows_bf[:, C:2 * C],
                                   start=True, stop=True)
                    M[("bcast", j)] = cpe.inc(ins)
                b1.tensor(t_bc)

                def v_norm(v, j=j):
                    v.wait_ge(cpe.sem, M[("bcast", j)])
                    ins = None
                    for f in range(NF):
                        v.tensor_tensor(xv(j, f), xv(j, f), pB1[:], ALU.mult)
                        ins = v.tensor_tensor(fsl(xn_sb, f), xv(j, f), pB2[:], ALU.subtract)
                    M[("xn", j)] = cve.inc(ins)
                    M[("xfree", j)] = cve.n
                b1.vector(v_norm)

                for h in range(HEADS):
                    def t_kfm(t, j=j, h=h):
                        pk = pK1 if h % 2 == 0 else pK2
                        t.wait_ge(cve.sem, M[("xn", j)])
                        if h >= 2:
                            t.wait_ge(csc.sem, M[("efm", j, h - 2)])
                        elif j >= 1:
                            t.wait_ge(cve.sem, M[("qh", j - 1, 3)])
                        ins = None
                        for f in range(NF):
                            ins = t.matmul(pk[:], wk_sb[:, f * 512 + 128 * h: f * 512 + 128 * (h + 1)],
                                           fsl(xn_sb, f), start=(f == 0), stop=(f == NF - 1))
                        M[("kfm", j, h)] = cpe.inc(ins)
                    b1.tensor(t_kfm)

                    def s_efm(s, j=j, h=h):
                        pk = pK1 if h % 2 == 0 else pK2
                        s.wait_ge(cpe.sem, M[("kfm", j, h)])
                        ins = s.activation(fsl(efm_sb, h), pk[:], AF.Exp)
                        M[("efm", j, h)] = csc.inc(ins)
                    b1.scalar(s_efm)

                for ts in range(4):
                    def t_ktm(t, j=j, ts=ts):
                        pk = pW1 if ts % 2 == 0 else pW2
                        if ts < 2:
                            t.wait_ge(csc.sem, M[("rowA", j)])
                        else:
                            t.wait_ge(csc.sem, M[("etm", j, ts - 2)])
                        ins = None
                        for f in range(NF):
                            ins = t.matmul(pk[:], xn_sb[:, f * C + 128 * ts: f * C + 128 * (ts + 1)],
                                           wk_sb[:, f * 512:(f + 1) * 512],
                                           start=(f == 0), stop=(f == NF - 1))
                        M[("ktm", j, ts)] = cpe.inc(ins)
                    b1.tensor(t_ktm)

                    def s_etm(s, j=j, ts=ts):
                        pk = pW1 if ts % 2 == 0 else pW2
                        s.wait_ge(cpe.sem, M[("ktm", j, ts)])
                        ins = s.activation(etm_sb[:, ts * 512:(ts + 1) * 512], pk[:], AF.Exp)
                        M[("etm", j, ts)] = csc.inc(ins)
                    b1.scalar(s_etm)

                for ts in range(4):
                    def t_vtm(t, j=j, ts=ts):
                        pk = pW3 if ts % 2 == 0 else pK1
                        if ts % 2 == 1:
                            t.wait_ge(csc.sem, M[("efm", j, 2)])
                        if ts >= 2:
                            t.wait_ge(cve.sem, M[("vev", j, ts - 2)])
                        ins = None
                        for f in range(NF):
                            ins = t.matmul(pk[:], xn_sb[:, f * C + 128 * ts: f * C + 128 * (ts + 1)],
                                           wv_sb[:, f * 512:(f + 1) * 512],
                                           start=(f == 0), stop=(f == NF - 1))
                        M[("vtm", j, ts)] = cpe.inc(ins)
                    b1.tensor(t_vtm)

                    def v_vev(v, j=j, ts=ts):
                        pk = pW3 if ts % 2 == 0 else pK1
                        v.wait_ge(cpe.sem, M[("vtm", j, ts)])
                        ins = v.tensor_copy(vtm_sb[:, ts * 512:(ts + 1) * 512], pk[:])
                        M[("vev", j, ts)] = cve.inc(ins)
                    b1.vector(v_vev)

                for h in range(HEADS):
                    def t_qfm(t, j=j, h=h):
                        pk = pK2 if h % 2 == 0 else pK1
                        if h == 0:
                            t.wait_ge(csc.sem, M[("efm", j, 3)])
                        elif h == 1:
                            t.wait_ge(cve.sem, M[("vev", j, 3)])
                        else:
                            t.wait_ge(cve.sem, M[("qh", j, h - 2)])
                        ins = None
                        for f in range(NF):
                            ins = t.matmul(pk[:], wq_sb[:, f * 512 + 128 * h: f * 512 + 128 * (h + 1)],
                                           fsl(xn_sb, f), start=(f == 0), stop=(f == NF - 1))
                        M[("qfm", j, h)] = cpe.inc(ins)
                    b1.tensor(t_qfm)

                    def s_texp(s, j=j, h=h):
                        pk = pK2 if h % 2 == 0 else pK1
                        s.wait_ge(cpe.sem, M[("qfm", j, h)])
                        if h >= 1:
                            s.wait_ge(cve.sem, M[("m1", j, h - 1)])
                        elif j >= 1:
                            s.wait_ge(cve.sem, M[("m1", j - 1, HEADS - 1)])
                        ins = s.activation(tq_sb[:], pk[:], AF.Exp, scale=-1.0)
                        M[("texp", j, h)] = csc.inc(ins)
                    b1.scalar(s_texp)

                    def t_zc(t, j=j, h=h):
                        t.wait_ge(csc.sem, M[("etm", j, 3)])
                        if h >= 1:
                            t.wait_ge(cve.sem, M[("zf", j, h - 1)])
                        elif j >= 1:
                            t.wait_ge(cve.sem, M[("zf", j - 1, HEADS - 1)])
                        ins = None
                        for ts in range(4):
                            ins = t.matmul(pW2[:], etm_sb[:, ts * 512 + 128 * h: ts * 512 + 128 * (h + 1)],
                                           u_sb[:, ts * C:(ts + 1) * C],
                                           start=(ts == 0), stop=(ts == 3))
                        M[("zc", j, h)] = cpe.inc(ins)
                    b1.tensor(t_zc)

                    def v_scan(v, j=j, h=h):
                        v.wait_ge(cpe.sem, M[("zc", j, h)])
                        ins = v.tensor_scalar(zf_sb[:], pW2[:], zcar_sb[:, h:h + 1],
                                              None, ALU.add)
                        M[("zf", j, h)] = cve.inc(ins)
                        ins = v.tensor_copy(zcar_sb[:, h:h + 1], zf_sb[:, C - 1:C])
                        cve.inc(ins)
                        v.wait_ge(csc.sem, M[("texp", j, h)])
                        ins = v.scalar_tensor_tensor(m1_sb[:], tq_sb[:], 1.0, zf_sb[:],
                                                     ALU.add, ALU.mult)
                        M[("m1", j, h)] = cve.inc(ins)
                        ins = v.reciprocal(rr_sb[:], m1_sb[:])
                        cve.inc(ins)
                        pq = pK2 if h % 2 == 0 else pK1
                        ins = v.scalar_tensor_tensor(fsl(qh_sb, h), pq[:], 1.0, rr_sb[:],
                                                     ALU.mult, ALU.mult)
                        M[("qh", j, h)] = cve.inc(ins)
                    b1.vector(v_scan)

                    for r in range(4):
                        def t_st(t, j=j, h=h, r=r):
                            pk = pW3 if r % 2 == 0 else pW1
                            t.wait_ge(cve.sem, M[("qh", j, h)])
                            if r >= 2:
                                t.wait_ge(cve.sem, M[("stev", j, h, r - 2)])
                            elif h >= 1:
                                t.wait_ge(cve.sem, M[("stev", j, h - 1, r + 2)])
                            elif r % 2 == 0:
                                t.wait_ge(cve.sem, M[("vev", j, 2)])
                            else:
                                t.wait_ge(csc.sem, M[("etm", j, 3)])
                            ins = t.matmul(pk[:], efm_sb[:, h * C + 128 * r: h * C + 128 * (r + 1)],
                                           fsl(qh_sb, h), start=True, stop=True)
                            M[("st", j, h, r)] = cpe.inc(ins)
                        b1.tensor(t_st)

                        def v_stev(v, j=j, h=h, r=r):
                            pk = pW3 if r % 2 == 0 else pW1
                            v.wait_ge(cpe.sem, M[("st", j, h, r)])
                            if r < 3:
                                v.tensor_copy(ssb_sb[:, r * C + 128 * (r + 1):(r + 1) * C],
                                              pk[:, 128 * (r + 1):C])
                            ins = v.tensor_tensor(
                                ssb_sb[:, r * C + 128 * r: r * C + 128 * (r + 1)],
                                pk[:, 128 * r:128 * (r + 1)],
                                u_sb[:, r * C + 128 * r: r * C + 128 * (r + 1)],
                                ALU.mult)
                            M[("stev", j, h, r)] = cve.inc(ins)
                        b1.vector(v_stev)

                    def t_o(t, j=j, h=h):
                        po = pB1 if h % 2 == 0 else pB2
                        if h < 2:
                            t.wait_ge(cve.sem, M[("xn", j)])
                        else:
                            t.wait_ge(csc.sem, M[("oev", j, h - 2)])
                        ins = t.matmul(po[:], abf_sb[:, h * 128:(h + 1) * 128],
                                       fsl(qh_sb, h), start=True, stop=False)
                        for ts in range(4):
                            t.wait_ge(cve.sem, M[("stev", j, h, ts)])
                            ins = t.matmul(po[:, 128 * ts:C],
                                           vtm_sb[:, ts * 512 + 128 * h: ts * 512 + 128 * (h + 1)],
                                           ssb_sb[:, ts * C + 128 * ts:(ts + 1) * C],
                                           start=False, stop=(ts == 3))
                        M[("omm", j, h)] = cpe.inc(ins)
                    b1.tensor(t_o)

                    def s_oev(s, j=j, h=h):
                        po = pB1 if h % 2 == 0 else pB2
                        s.wait_ge(cpe.sem, M[("omm", j, h)])
                        if o_store_cnt[0] >= 2:
                            s.wait_ge(s_o, 16 * (o_store_cnt[0] - 1))
                        ins = s.activation(osb_sb[:, (h % 2) * 512:(h % 2 + 1) * 512],
                                           po[:], AF.Copy, scale=SCALE_O)
                        M[("oev", j, h)] = csc.inc(ins)
                    b1.scalar(s_oev)

                    def g_ost(g, j=j, h=h):
                        g.wait_ge(csc.sem, M[("oev", j, h)])
                        r0 = (j * HEADS + h) * 128
                        g.dma_start(out=o_dram[r0:r0 + 128, :],
                                    in_=osb_sb[:, (h % 2) * 512:(h % 2 + 1) * 512]
                                    ).then_inc(s_o, 16)
                        o_store_cnt[0] += 1
                    b1.gpsimd(g_ost)

                    def t_au(t, j=j, h=h):
                        if h >= 1:
                            t.wait_ge(cve.sem, M[("abf", j, h - 1)])
                        elif j >= 1:
                            t.wait_ge(cve.sem, M[("abf", j - 1, HEADS - 1)])
                        ins = None
                        for ts in range(4):
                            ins = t.matmul(pA[:, h * 128:(h + 1) * 128],
                                           etm_sb[:, ts * 512 + 128 * h: ts * 512 + 128 * (h + 1)],
                                           vtm_sb[:, ts * 512 + 128 * h: ts * 512 + 128 * (h + 1)],
                                           start=(j == 0 and ts == 0),
                                           stop=(j == NCH - 1 and ts == 3))
                        M[("aupd", j, h)] = cpe.inc(ins)
                    b1.tensor(t_au)

                    def v_abf(v, j=j, h=h):
                        v.wait_ge(cpe.sem, M[("aupd", j, h)])
                        ins = v.tensor_copy(abf_sb[:, h * 128:(h + 1) * 128],
                                            pA[:, h * 128:(h + 1) * 128])
                        M[("abf", j, h)] = cve.inc(ins)
                    b1.vector(v_abf)

                if j < 2 * NHC:
                    def g_cast(g, j=j):
                        par = j % 2
                        src = wgTd if j < NHC else wuTd
                        dst = wgbf if j < NHC else wubf
                        hp = j % NHC
                        if j >= 2:
                            g.wait_ge(s_pcw, 16 * (j - 1))
                        g.dma_start(
                            out=cast_sb[:, par * NF * HC:(par + 1) * NF * HC]
                            .rearrange("p (f h) -> p f h", h=HC),
                            in_=src.rearrange("(f p) h -> p f h", p=128)[:, :, hp * HC:(hp + 1) * HC],
                        ).then_inc(s_pcr, 16)
                        g.wait_ge(s_pcr, 16 * (j + 1))
                        g.dma_start(
                            out=dst[hp * 128:(hp + 1) * 128, :],
                            in_=cast_sb[:, par * NF * HC:(par + 1) * NF * HC],
                        ).then_inc(s_pcw, 16)
                    b1.gpsimd(g_cast)

            def g_cc(g):
                g.wait_ge(s_o, 16 * NCH * HEADS)
                g.collective_compute(
                    "AllGather", ALU.bypass, replica_groups=groups,
                    ins=[o_dram.ap()], outs=[cc_out.ap()],
                ).then_inc(s_cc, 1)
            b1.gpsimd(g_cc)

            def s_fin1(s):
                s.wait_ge(cpe.sem, cpe.n)
                s.wait_ge(s_x, 16 * NCH)
                s.wait_ge(s_pcw, 16 * 2 * NHC)
                s.wait_ge(s_lnr, 16 * NCH)
                s.copy(rows_sb[0:1, 0:1], rows_sb[0:1, 0:1]).then_inc(s_p1, 1)
            b1.scalar(s_fin1)

            def v_fin1(v):
                v.engine_nop().then_inc(s_p1, 1)
            b1.vector(v_fin1)

            def g_fin1(g):
                g.engine_nop().then_inc(s_p1, 1)
            b1.gpsimd(g_fin1)

        es2 = contextlib.ExitStack()
        with es2:
            sb2 = lambda *a: es2.enter_context(nc.sbuf_tensor(*a))
            ps2 = lambda *a: es2.enter_context(nc.psum_tensor(*a))
            x2_sb = sb2("x2_sb", [128, 2 * NF * TS], F32)
            xt_sb = sb2("xt_sb", [128, TS], F32)
            xn2_sb = sb2("xn2_sb", [128, NF * TS], BF16)
            ob2_sb = sb2("ob2_sb", [128, 2 * NF * 512], BF16)
            wgs_sb = sb2("wgs_sb", [128, 2 * NF * HC], BF16)
            wus_sb = sb2("wus_sb", [128, 2 * NF * HC], BF16)
            h_sb = sb2("h_sb", [128, (HID // 128) * TS], BF16)
            g1_sb = sb2("g1_sb", [128, TS], BF16)
            th_sb = sb2("th_sb", [128, TS], F32)
            gsig_sb = sb2("gsig_sb", [128, TS], BF16)
            r2_sb = sb2("r2_sb", [1, TS], F32)
            osq_sb = sb2("osq_sb", [128, 2 * 512], BF16)
            gs_sb = sb2("gs_sb", [128, 2 * TS], F32)
            o2_sb = sb2("o2_sb", [128, 2 * 512], F32)
            lnr2_sb = sb2("lnr2_sb", [128, 2 * 2 * C], BF16)
            qB1 = ps2("qB1", [128, TS], F32)
            qB2 = ps2("qB2", [128, TS], F32)
            qG1 = ps2("qG1", [128, TS], F32)
            qG2 = ps2("qG2", [128, TS], F32)
            qG3 = ps2("qG3", [128, TS], F32)
            qG4 = ps2("qG4", [128, TS], F32)
            qO1 = ps2("qO1", [128, TS], F32)
            qO2 = ps2("qO2", [128, TS], F32)
            b2 = es2.enter_context(nc.Block())

            def f2(buf, f, w=TS):
                return buf[:, f * w:(f + 1) * w]

            def xv2(t, f):
                base = (t % 2) * NF * TS
                return x2_sb[:, base + f * TS: base + (f + 1) * TS]

            st_cnt = [0]

            def emit_stream(k, wait_key=None):
                tp, hp = divmod(k, NHC)
                def q_wg(q, hp=hp, par=k % 2, wait_key=wait_key):
                    if wait_key is not None:
                        q.wait_ge(cpe.sem, M[wait_key])
                    q.dma_start(out=wgs_sb[:, par * NF * HC:(par + 1) * NF * HC],
                                in_=wgbf[hp * 128:(hp + 1) * 128, :]).then_inc(s_wg, 16)
                b2.sync(q_wg)
                def q_wu(q, hp=hp, par=k % 2, wait_key=wait_key):
                    if wait_key is not None:
                        q.wait_ge(cpe.sem, M[wait_key])
                    q.dma_start(out=wus_sb[:, par * NF * HC:(par + 1) * NF * HC],
                                in_=wubf[hp * 128:(hp + 1) * 128, :]).then_inc(s_wu, 16)
                b2.gpsimd(q_wu)

            def barrier(e):
                e.wait_ge(s_p1, 3)
            for sec in (b2.sync, b2.tensor, b2.scalar, b2.vector, b2.gpsimd):
                sec(barrier)
            emit_stream(0)
            emit_stream(1)

            def emit_lnr(t):
                def g_lnr(g, t=t):
                    if t >= 2:
                        g.wait_ge(cve.sem, M[("xn2", t - 2)])
                    g.indirect_dma_start(
                        lnr2_sb[:, (t % 2) * 1024:(t % 2 + 1) * 1024],
                        None,
                        rows_dram[:, :], IndirectOffsetOnAxis(ridx_sb[:, t:t + 1], 0),
                    ).then_inc(s_lr, 16)
                b2.gpsimd(g_lnr)

            def norm2(t):
                def v_norm(v, t=t):
                    par = t % 2
                    v.wait_ge(s_lr, 16 * (t + 1))
                    v.wait_ge(s_x2, 16 * (t + 1))
                    if t >= 1:
                        v.wait_ge(cpe.sem, M[("ummx", t - 1, NHC - 1, 3)])
                    ins = None
                    for f in range(NF):
                        v.tensor_tensor(xt_sb[:], xv2(t, f),
                                        lnr2_sb[:, par * 1024: par * 1024 + 512], ALU.mult)
                        ins = v.tensor_tensor(f2(xn2_sb, f), xt_sb[:],
                                              lnr2_sb[:, par * 1024 + 512:(par + 1) * 1024],
                                              ALU.subtract)
                    M[("xn2", t)] = cve.inc(ins)
                b2.vector(v_norm)

            norm2(0)

            for t_i in range(NTS):
                def s_load2(s, t_i=t_i):
                    if t_i >= 2:
                        s.wait_ge(cve.sem, M[("fadd", t_i - 2, NF - 1)])
                    s.dma_start(
                        out=x2_sb[:, (t_i % 2) * NF * TS:((t_i % 2) + 1) * NF * TS]
                        .rearrange("p (f t) -> p f t", t=TS),
                        in_=xoT.rearrange("(f p) t -> p f t", p=128)[:, :, t_i * TS:(t_i + 1) * TS],
                    ).then_inc(s_x2, 16)
                b2.sync(s_load2)

                if t_i == 0:
                    emit_lnr(0)
                    emit_lnr(1)
                elif t_i + 1 < NTS:
                    emit_lnr(t_i + 1)

                def g_og(g, t_i=t_i):
                    if t_i == 0:
                        g.wait_ge(s_cc, 1)
                    if t_i >= 2:
                        g.wait_ge(cpe.sem, M[("dw", t_i - 2, NF - 1)])
                    for H in range(8):
                        col = t_i * 8 + H
                        g.indirect_dma_start(
                            ob2_sb[:, ((t_i % 2) * NF + H) * 512:((t_i % 2) * NF + H + 1) * 512],
                            None,
                            cc_out[:, :], IndirectOffsetOnAxis(oidx_sb[:, col:col + 1], 0),
                        ).then_inc(s_og, 16)
                b2.gpsimd(g_og)

                for hc in range(NHC):
                    for hf in range(4):
                        def t_g(t, t_i=t_i, hc=hc, hf=hf):
                            k = t_i * NHC + hc
                            par = k % 2
                            pg = qG1 if hf % 2 == 0 else qG2
                            t.wait_ge(cve.sem, M[("xn2", t_i)])
                            t.wait_ge(s_wg, 16 * (k + 1))
                            if (t_i, hc, hf) > (0, 0, 1):
                                prev = (t_i, hc, hf - 2) if hf >= 2 else \
                                       ((t_i, hc - 1, hf + 2) if hc >= 1 else (t_i - 1, NHC - 1, hf + 2))
                                t.wait_ge(csc.sem, M[("silu",) + prev])
                            ins = None
                            for f in range(NF):
                                ins = t.matmul(
                                    pg[:],
                                    wgs_sb[:, par * NF * HC + f * HC + 128 * hf:
                                           par * NF * HC + f * HC + 128 * (hf + 1)],
                                    f2(xn2_sb, f), start=(f == 0), stop=(f == NF - 1))
                            M[("gmm", t_i, hc, hf)] = cpe.inc(ins)
                        b2.tensor(t_g)

                        def s_silu(s, t_i=t_i, hc=hc, hf=hf):
                            pg = qG1 if hf % 2 == 0 else qG2
                            s.wait_ge(cpe.sem, M[("gmm", t_i, hc, hf)])
                            kt = hc * 4 + hf
                            kprev = t_i * 24 + kt - 2
                            if kprev >= 0:
                                tp, kp = divmod(kprev, 24)
                                s.wait_ge(cve.sem, M[("hst", tp, kp // 4, kp % 4)])
                            par = kt % 2
                            ins = s.activation(gs_sb[:, par * TS:(par + 1) * TS], pg[:],
                                               AF.Silu, scale=Q127)
                            M[("silu", t_i, hc, hf)] = csc.inc(ins)
                        b2.scalar(s_silu)

                        def t_u(t, t_i=t_i, hc=hc, hf=hf):
                            k = t_i * NHC + hc
                            par = k % 2
                            pu = qG3 if hf % 2 == 0 else qG4
                            t.wait_ge(s_wu, 16 * (k + 1))
                            if (t_i, hc, hf) > (0, 0, 1):
                                prev = (t_i, hc, hf - 2) if hf >= 2 else \
                                       ((t_i, hc - 1, hf + 2) if hc >= 1 else (t_i - 1, NHC - 1, hf + 2))
                                t.wait_ge(cve.sem, M[("hst",) + prev])
                            ins = None
                            for f in range(NF):
                                ins = t.matmul(
                                    pu[:],
                                    wus_sb[:, par * NF * HC + f * HC + 128 * hf:
                                           par * NF * HC + f * HC + 128 * (hf + 1)],
                                    f2(xn2_sb, f), start=(f == 0), stop=(f == NF - 1))
                            M[("umm", t_i, hc)] = M[("ummx", t_i, hc, hf)] = cpe.inc(ins)
                        b2.tensor(t_u)

                        def v_h(v, t_i=t_i, hc=hc, hf=hf):
                            pu = qG3 if hf % 2 == 0 else qG4
                            v.wait_ge(cpe.sem, M[("ummx", t_i, hc, hf)])
                            v.wait_ge(csc.sem, M[("silu", t_i, hc, hf)])
                            kt = hc * 4 + hf
                            ins = v.scalar_tensor_tensor(h_sb[:, kt * TS:(kt + 1) * TS],
                                                         pu[:], Q127SQ,
                                                         gs_sb[:, (kt % 2) * TS:(kt % 2 + 1) * TS],
                                                         ALU.mult, ALU.mult)
                            M[("hst", t_i, hc, hf)] = cve.inc(ins)
                        b2.vector(v_h)
                        if hf == 3:
                            knext = (t_i * NHC + hc) + 2
                            if knext < NTS * NHC:
                                emit_stream(knext, wait_key=("umm", t_i, hc))

                    if hc == 0:
                        def t_g1(t, t_i=t_i):
                            t.wait_ge(cve.sem, M[("xn2", t_i)])
                            if t_i >= 1:
                                t.wait_ge(cve.sem, M[("gsig", t_i - 1, 7)])
                            ins = None
                            for f in range(NF):
                                ins = t.matmul(qO2[:, 0:TS], wg1_sb[:, f * 128:(f + 1) * 128],
                                               f2(xn2_sb, f), start=(f == 0), stop=(f == NF - 1))
                            M[("g1", t_i)] = cpe.inc(ins)
                        b2.tensor(t_g1)

                        def s_g1(s, t_i=t_i):
                            s.wait_ge(cpe.sem, M[("g1", t_i)])
                            ins = s.copy(g1_sb[:], qO2[:, 0:TS])
                            M[("g1c", t_i)] = csc.inc(ins)
                        b2.scalar(s_g1)

                    if hc in (1, 2):
                        for gf in range(4 * (hc - 1), 4 * hc):
                            def s_osq1(s, t_i=t_i, gf=gf):
                                base = (t_i % 2) * NF * 512
                                if gf == 0:
                                    s.wait_ge(s_og, 16 * 8 * (t_i + 1))
                                if gf >= 2:
                                    s.wait_ge(cpe.sem, M[("osmm", t_i, gf - 2)])
                                par = gf % 2
                                ins = s.activation(osq_sb[:, par * 512:(par + 1) * 512],
                                                   ob2_sb[:, base + gf * 512: base + (gf + 1) * 512],
                                                   AF.Square)
                                M[("osq", t_i, gf)] = csc.inc(ins)
                            b2.scalar(s_osq1)

                            def t_osmm(t, t_i=t_i, gf=gf):
                                t.wait_ge(csc.sem, M[("osq", t_i, gf)])
                                if gf == 0 and t_i >= 1:
                                    t.wait_ge(cve.sem, M[("fadd", t_i - 1, 6)])
                                par = gf % 2
                                ins = t.matmul(qO1[0:1, 0:TS], ones_m1[:],
                                               osq_sb[:, par * 512:(par + 1) * 512],
                                               start=(gf == 0), stop=(gf == 7))
                                M[("osmm", t_i, gf)] = cpe.inc(ins)
                            b2.tensor(t_osmm)

                    if hc == 2:
                        def s_rmsrow(s, t_i=t_i):
                            s.wait_ge(cpe.sem, M[("osmm", t_i, 7)])
                            s.activation(r2_sb[:], qO1[0:1, 0:TS],
                                         AF.Ln, bias=eps_sb[:], scale=1.0 / D)
                            ins = s.activation(rows_bf[:, C:C + TS], r2_sb[:],
                                               AF.Exp, scale=-0.5)
                            M[("rstd2r", t_i)] = csc.inc(ins)
                        b2.scalar(s_rmsrow)

                        def t_rmsbc(t, t_i=t_i):
                            t.wait_ge(csc.sem, M[("rstd2r", t_i)])
                            ins = t.matmul(qO1[:], ones_k1[:], rows_bf[:, C:C + TS],
                                           start=True, stop=True)
                            M[("rmsbc", t_i)] = cpe.inc(ins)
                        b2.tensor(t_rmsbc)

                    if hc >= 3:
                        for gf in ((0, 1, 2), (3, 4, 5), (6, 7))[hc - 3]:
                            def t_gate(t, t_i=t_i, gf=gf):
                                pg = qB2 if gf % 2 == 0 else qO2
                                if gf <= 1:
                                    t.wait_ge(csc.sem, M[("g1c", t_i)])
                                if gf == 0 and t_i >= 1:
                                    t.wait_ge(cve.sem, M[("fadd", t_i - 1, NF - 1)])
                                if gf >= 2:
                                    t.wait_ge(cve.sem, M[("gsig", t_i, gf - 2)])
                                ins = t.matmul(pg[:, 0:TS], wg2_sb[:, gf * 128:(gf + 1) * 128],
                                               g1_sb[:], start=True, stop=True)
                                M[("gate", t_i, gf)] = cpe.inc(ins)
                            b2.tensor(t_gate)

                            def s_tanh(s, t_i=t_i, gf=gf):
                                pg = qB2 if gf % 2 == 0 else qO2
                                s.wait_ge(cpe.sem, M[("gate", t_i, gf)])
                                if gf >= 1:
                                    s.wait_ge(cve.sem, M[("gsig", t_i, gf - 1)])
                                ins = s.activation(th_sb[:], pg[:, 0:TS], AF.Tanh, scale=0.5)
                                M[("tanh", t_i, gf)] = csc.inc(ins)
                            b2.scalar(s_tanh)

                            def v_gsig(v, t_i=t_i, gf=gf):
                                pg = qB2 if gf % 2 == 0 else qO2
                                v.wait_ge(csc.sem, M[("tanh", t_i, gf)])
                                ins = v.scalar_tensor_tensor(gsig_sb[:], th_sb[:],
                                                             1.0, pg[:, 0:TS], ALU.add, ALU.mult)
                                M[("gsig", t_i, gf)] = cve.inc(ins)
                                if gf == 0:
                                    v.wait_ge(cpe.sem, M[("rmsbc", t_i)])
                                base = (t_i % 2) * NF * 512
                                sl = ob2_sb[:, base + gf * 512: base + (gf + 1) * 512]
                                v.tensor_tensor(sl, sl, qO1[:], ALU.mult)
                                ins = v.tensor_tensor(sl, sl, gsig_sb[:], ALU.mult)
                                M[("ogt", t_i, gf)] = cve.inc(ins)
                            b2.vector(v_gsig)

                if t_i + 1 < NTS:
                    norm2(t_i + 1)

                for dt in range(NF):
                    def t_dw(t, t_i=t_i, dt=dt):
                        po = qO1 if dt % 2 == 0 else qB2
                        if dt == 0:
                            t.wait_ge(cve.sem, M[("ogt", t_i, 7)])
                            t.wait_ge(cve.sem, M[("hst", t_i, NHC - 1, 3)])
                        if dt == 1:
                            t.wait_ge(cve.sem, M[("gsig", t_i, 6)])
                        if dt >= 2:
                            t.wait_ge(cve.sem, M[("fadd", t_i, dt - 2)])
                        ins = None
                        nk = HID // 128
                        base = (t_i % 2) * NF * 512
                        for kt in range(nk):
                            ins = t.matmul(po[:, 0:TS],
                                           wd_sb[:, kt * D + 128 * dt: kt * D + 128 * (dt + 1)],
                                           h_sb[:, kt * TS:(kt + 1) * TS],
                                           start=(kt == 0), stop=False)
                        for of in range(NF):
                            ins = t.matmul(po[:, 0:TS],
                                           wo_sb[:, of * D + 128 * dt: of * D + 128 * (dt + 1)],
                                           ob2_sb[:, base + of * 512: base + (of + 1) * 512],
                                           start=False, stop=(of == NF - 1))
                        M[("dw", t_i, dt)] = cpe.inc(ins)
                    b2.tensor(t_dw)

                    def v_fadd(v, t_i=t_i, dt=dt):
                        po = qO1 if dt % 2 == 0 else qB2
                        v.wait_ge(cpe.sem, M[("dw", t_i, dt)])
                        if st_cnt[0] >= 2:
                            v.wait_ge(s_st, 16 * (st_cnt[0] - 1))
                        ins = v.tensor_tensor(o2_sb[:, (dt % 2) * 512:(dt % 2) * 512 + TS],
                                              xv2(t_i, dt), po[:, 0:TS], ALU.add)
                        M[("fadd", t_i, dt)] = cve.inc(ins)
                    b2.vector(v_fadd)

                    def s_store(s, t_i=t_i, dt=dt):
                        s.wait_ge(cve.sem, M[("fadd", t_i, dt)])
                        s.dma_start(
                            out=out.rearrange("(f p) t -> p f t", p=128)[:, dt:dt + 1,
                                                                         t_i * TS:(t_i + 1) * TS],
                            in_=o2_sb[:, (dt % 2) * 512:(dt % 2) * 512 + TS],
                        ).then_inc(s_st, 16)
                        st_cnt[0] += 1
                    b2.scalar(s_store)

            def s_fin(s):
                s.wait_ge(s_st, 16 * NTS * NF)
            b2.scalar(s_fin)

    return nc



def _prep_core_inputs(c, x, Wq, Wk, Wv, Wg1, Wg2, g_norm, Wo, wg, wu, wd, T_full, T_own):
    NCH = T_full // C
    NTS = T_own // TS
    b, hg = c // 2, c % 2
    half = slice(hg * T_own, (hg + 1) * T_own)
    cT = np.ascontiguousarray
    U = np.triu(np.ones((C, C), dtype=np.float32))
    oidx = np.zeros((128, NTS * 8), dtype=np.int32)
    for t_i in range(NTS):
        for H in range(8):
            slot, lh = H // 4, H % 4
            j = hg * (NCH // 2) + t_i
            row0 = slot * (NCH * HEADS * 128) + (j * HEADS + lh) * 128
            oidx[:, t_i * 8 + H] = row0 + np.arange(128)
    ridx = np.tile(np.array([hg * (NCH // 2) + t for t in range(NTS)],
                            dtype=np.int32), (128, 1))
    return {
        "xT": cT(x[b].T),
        "xoT": cT(x[b, half].T),
        "wqT": cT(Wq.T[:, hg * 512:(hg + 1) * 512]),
        "wkT": cT(Wk.T[:, hg * 512:(hg + 1) * 512]),
        "wvT": cT(Wv.T[:, hg * 512:(hg + 1) * 512]),
        "woT": cT(Wo.T),
        "wg1T": cT(Wg1.T),
        "wg2T": cT(Wg2.T),
        "gnorm": cT(g_norm.reshape(NF, 128).T),
        "wgTd": cT(wg.T),
        "wuTd": cT(wu.T),
        "wdTd": cT(wd.T),
        "umask": U,
        "oidx": oidx,
        "ridx": ridx,
    }


def kernel(L0_attn_hidden_states, L0_attn_Wq, L0_attn_Wk, L0_attn_Wv,
           L0_attn_Wg1, L0_attn_Wg2, L0_attn_g_norm_weight, L0_attn_Wo,
           L0_ffn_w_gate, L0_ffn_w_up, L0_ffn_w_down):
    x = np.asarray(L0_attn_hidden_states, dtype=np.float32)
    B, T, _ = x.shape
    T_own = T // 2
    args = (np.asarray(L0_attn_Wq, np.float32), np.asarray(L0_attn_Wk, np.float32),
            np.asarray(L0_attn_Wv, np.float32), np.asarray(L0_attn_Wg1, np.float32),
            np.asarray(L0_attn_Wg2, np.float32), np.asarray(L0_attn_g_norm_weight, np.float32),
            np.asarray(L0_attn_Wo, np.float32), np.asarray(L0_ffn_w_gate, np.int32),
            np.asarray(L0_ffn_w_up, np.int32), np.asarray(L0_ffn_w_down, np.int32))
    nc = build_program(T, T_own)
    in_maps = [_prep_core_inputs(c, x, *args, T, T_own) for c in range(8)]
    res = run_bass_kernel_spmd(nc, in_maps, core_ids=list(range(8)))
    outp = np.empty((B, T, D), dtype=np.float32)
    for c in range(8):
        b, hg = c // 2, c % 2
        outp[b, hg * T_own:(hg + 1) * T_own, :] = res.results[c]["out"].T
    return outp


# version 16
# speedup vs baseline: 1.3321x; 1.3321x over previous
import contextlib
import numpy as np
import ml_dtypes
import concourse.bass as bass
import concourse.mybir as mybir
from concourse.bass_utils import run_bass_kernel_spmd
from concourse.bass_core_rust import IndirectOffsetOnAxis

F32 = mybir.dt.float32
BF16 = mybir.dt.bfloat16
I32 = mybir.dt.int32
AF = mybir.ActivationFunctionType
ALU = mybir.AluOpType

D = 1024
NF = 8
HEADS = 4
C = 512
TS = 512
HID = 3072
HC = 512
NHC = HID // HC
EPS = 1e-5
JSPLIT = 12
SCALE_O = float(128) ** -0.5
Q127 = 1.0 / 127.0
Q127SQ = 1.0 / (127.0 * 127.0)


class Ctr:
    def __init__(self, sem):
        self.sem = sem
        self.n = 0

    def inc(self, inst, k=1):
        inst.then_inc(self.sem, k)
        self.n += k
        return self.n


def build_program(T_full, T_own, dbg=False):
    NCH = T_full // C
    NTS = T_own // TS
    NROW = NCH * HEADS * 128

    nc = bass.Bass("TRN2", target_bir_lowering=False)

    xT = nc.dram_tensor("xT", [D, T_full], F32, kind="ExternalInput").ap()
    xoT = nc.dram_tensor("xoT", [D, T_own], F32, kind="ExternalInput").ap()
    wqT = nc.dram_tensor("wqT", [D, 512], F32, kind="ExternalInput").ap()
    wkT = nc.dram_tensor("wkT", [D, 512], F32, kind="ExternalInput").ap()
    wvT = nc.dram_tensor("wvT", [D, 512], F32, kind="ExternalInput").ap()
    woT = nc.dram_tensor("woT", [D, D], F32, kind="ExternalInput").ap()
    wg1T = nc.dram_tensor("wg1T", [D, 128], F32, kind="ExternalInput").ap()
    wg2T = nc.dram_tensor("wg2T", [128, D], F32, kind="ExternalInput").ap()
    gnorm = nc.dram_tensor("gnorm", [128, NF], F32, kind="ExternalInput").ap()
    wgTd = nc.dram_tensor("wgTd", [D, HID], I32, kind="ExternalInput").ap()
    wuTd = nc.dram_tensor("wuTd", [D, HID], I32, kind="ExternalInput").ap()
    wdTd = nc.dram_tensor("wdTd", [HID, D], I32, kind="ExternalInput").ap()
    umask = nc.dram_tensor("umask", [C, C], F32, kind="ExternalInput").ap()
    ident = nc.dram_tensor("ident", [128, 128], F32, kind="ExternalInput").ap()
    oidx = nc.dram_tensor("oidx", [128, NTS * 8], I32, kind="ExternalInput").ap()
    ridx = nc.dram_tensor("ridx", [128, NTS], I32, kind="ExternalInput").ap()
    out = nc.dram_tensor("out", [D, T_own], F32, kind="ExternalOutput").ap()

    NROW0 = JSPLIT * HEADS * 128
    NROW1 = (NCH - JSPLIT) * HEADS * 128
    o0_dram = nc.dram_tensor("o0_dram", [NROW0, 512], BF16, kind="Internal")
    o1_dram = nc.dram_tensor("o1_dram", [NROW1, 512], BF16, kind="Internal")
    cc0 = nc.dram_tensor("cc0", [2 * NROW0, 512], BF16, kind="Internal")
    cc1 = nc.dram_tensor("cc1", [2 * NROW1, 512], BF16, kind="Internal")
    wgbf = nc.dram_tensor("wgbf", [NHC * 128, NF * HC], BF16, kind="Internal").ap()
    wubf = nc.dram_tensor("wubf", [NHC * 128, NF * HC], BF16, kind="Internal").ap()
    rows_dram = nc.dram_tensor("rows_dram", [NCH, 2 * C], BF16, kind="Internal")
    groups = [[0, 1], [2, 3], [4, 5], [6, 7]]

    es = contextlib.ExitStack()
    with es:
        sb = lambda *a: es.enter_context(nc.sbuf_tensor(*a))
        ps = lambda *a: es.enter_context(nc.psum_tensor(*a))
        sem = lambda n: es.enter_context(nc.semaphore(n))
        wd_sb = sb("wd_sb", [128, (HID // 128) * D], BF16)
        wo_sb = sb("wo_sb", [128, NF * D], BF16)
        wg1_sb = sb("wg1_sb", [128, NF * 128], BF16)
        wg2_sb = sb("wg2_sb", [128, D], BF16)
        gn_sb = sb("gn_sb", [128, NF], F32)
        oidx_sb = sb("oidx_sb", [128, NTS * 8], I32)
        ridx_sb = sb("ridx_sb", [128, NTS], I32)
        ones_k1 = sb("ones_k1", [1, 128], BF16)
        ones_m1 = sb("ones_m1", [128, 1], BF16)
        rows_bf = sb("rows_bf", [1, 2 * C], BF16)
        eps_sb = sb("eps_sb", [1, 1], F32)
        lnso_sb = sb("lnso_sb", [128, 1], F32)
        s_x = sem("s_x"); s_o = sem("s_o"); s_cc = sem("s_cc")
        s_x2 = sem("s_x2"); s_wg = sem("s_wg"); s_og = sem("s_og")
        s_st = sem("s_st"); s_ld = sem("s_ld")
        s_wu = sem("s_wu"); s_pcr = sem("s_pcr"); s_pcw = sem("s_pcw")
        s_lnr = sem("s_lnr"); s_lr = sem("s_lr")
        _pe = sem("c_pe"); _sc = sem("c_sc"); _ve = sem("c_ve")
        s_p1 = sem("s_p1")
        cpe, csc, cve = Ctr(_pe), Ctr(_sc), Ctr(_ve)
        M = {}
        nsetup = [0]

        with nc.Block() as b0:
            def g_setup(g):
                loads = [
                    (wd_sb[:].rearrange("p (kt d) -> p kt d", d=D),
                     wdTd.rearrange("(kt p) d -> p kt d", p=128)),
                    (wo_sb[:].rearrange("p (f d) -> p f d", d=D),
                     woT.rearrange("(f p) d -> p f d", p=128)),
                    (wg1_sb[:].rearrange("p (f l) -> p f l", l=128),
                     wg1T.rearrange("(f p) l -> p f l", p=128)),
                    (wg2_sb[:], wg2T),
                    (gn_sb[:], gnorm),
                    (oidx_sb[:], oidx),
                    (ridx_sb[:], ridx),
                ]
                for dst, src in loads:
                    g.dma_start(out=dst, in_=src).then_inc(s_ld, 16)
                    nsetup[0] += 1
                g.memset(eps_sb[:], EPS)
                g.memset(lnso_sb[:], float(np.log(SCALE_O)))
                g.memset(ones_k1[:], 1.0)
                g.memset(ones_m1[:], 1.0)
            b0.gpsimd(g_setup)

            def v_setup(v):
                v.wait_ge(s_ld, 16 * nsetup[0])
                ins = None
                for f in range(NF):
                    ins = v.tensor_scalar(
                        wo_sb[:, f * D:(f + 1) * D], wo_sb[:, f * D:(f + 1) * D],
                        gn_sb[:, f:f + 1], 0.5, ALU.mult, ALU.mult)
                M["wo_scaled"] = cve.inc(ins)
            b0.vector(v_setup)

        es1 = contextlib.ExitStack()
        with es1:
            sb1 = lambda *a: es1.enter_context(nc.sbuf_tensor(*a))
            ps1 = lambda *a: es1.enter_context(nc.psum_tensor(*a))
            wq_sb = sb1("wq_sb", [128, NF * 512], BF16)
            wk_sb = sb1("wk_sb", [128, NF * 512], BF16)
            wv_sb = sb1("wv_sb", [128, NF * 512], BF16)
            x_sb = sb1("x_sb", [128, 2 * NF * C], F32)
            xbf_sb = sb1("xbf_sb", [128, NF * C], BF16)
            xn_sb = sb1("xn_sb", [128, NF * C], BF16)
            efm_sb = sb1("efm_sb", [128, HEADS * C], BF16)
            etm_sb = sb1("etm_sb", [128, HEADS * 4 * 128], BF16)
            vtm_sb = sb1("vtm_sb", [128, 4 * 512], BF16)
            sq4_sb = sb1("sq4_sb", [128, HEADS * C], BF16)
            qh_sb = sb1("qh_sb", [128, HEADS * C], BF16)
            ssb_sb = sb1("ssb_sb", [128, 4 * C], BF16)
            rr_sb = sb1("rr_sb", [128, 2 * C], F32)
            zf_sb = sb1("zf_sb", [128, 2 * C], F32)
            dum_sb = sb1("dum_sb", [1, 1], F32)
            zcar_sb = sb1("zcar_sb", [128, HEADS], F32)
            abf_sb = sb1("abf_sb", [128, HEADS * 128], BF16)
            osb_sb = sb1("osb_sb", [128, 2 * 512], BF16)
            cast_sb = sb1("cast_sb", [128, 2 * NF * HC], BF16)
            u_sb = sb1("u_sb", [128, (C // 128) * C], BF16)
            rows_sb = sb1("rows_sb", [1, 4 * C], F32)
            ident_sb = sb1("ident_sb", [128, 128], BF16)
            pA = ps1("pA", [128, 512], F32)
            oA = ps1("oA", [128, C], F32)
            oB = ps1("oB", [128, C], F32)
            sA = ps1("sA", [128, C], F32)
            sB = ps1("sB", [128, C], F32)
            pZ = ps1("pZ", [128, C], F32)
            pS = ps1("pS", [128, C], F32)
            pT = ps1("pT", [128, 4 * 128], BF16)
            b1 = es1.enter_context(nc.Block())

            def xv(j, f):
                base = (j % 2) * NF * C
                return x_sb[:, base + f * C: base + f * C + C]

            def fsl(buf, f, w=C):
                return buf[:, f * w:(f + 1) * w]

            def eslot(h, ts):
                return etm_sb[:, (h * 4 + ts) * 128:(h * 4 + ts + 1) * 128]

            def g_qkv(g):
                for dst, src in ((wq_sb, wqT), (wk_sb, wkT), (wv_sb, wvT)):
                    g.dma_start(
                        out=dst[:].rearrange("p (f i) -> p f i", i=512),
                        in_=src.rearrange("(f p) i -> p f i", p=128),
                    ).then_inc(s_ld, 16)
                g.dma_start(
                    out=u_sb[:].rearrange("p (s t) -> p s t", t=C),
                    in_=umask.rearrange("(s p) t -> p s t", p=128),
                ).then_inc(s_ld, 16)
                g.dma_start(out=ident_sb[:], in_=ident).then_inc(s_ld, 16)
                nsetup[0] += 5
            b1.gpsimd(g_qkv)

            def v_init(v):
                v.wait_ge(s_ld, 16 * nsetup[0])
                v.memset(zcar_sb[:], 0.0)
                ins = v.memset(abf_sb[:], 0.0)
                M["p1_init"] = cve.inc(ins)
            b1.vector(v_init)

            o_store_cnt = [0]

            def emit_load(jj):
                def s_load(s, jj=jj):
                    if jj >= 2:
                        s.wait_ge(cve.sem, M[("xfree", jj - 2)])
                    s.dma_start(
                        out=x_sb[:, (jj % 2) * NF * C:(jj % 2 + 1) * NF * C]
                        .rearrange("p (f t) -> p f t", t=C),
                        in_=xT.rearrange("(f p) t -> p f t", p=128)[:, :, jj * C:(jj + 1) * C],
                    ).then_inc(s_x, 16)
                b1.sync(s_load)

            def bcast_pair(a3, b2d):
                b3 = b2d.rearrange("p (a t) -> p a t", a=1)
                return bass.broadcast_tensor_aps(a3, b3)

            def prep_ln_a(jj):
                def v_xbf(v, jj=jj):
                    v.wait_ge(s_x, 16 * (jj + 1))
                    if jj >= 1:
                        v.wait_ge(cpe.sem, M[("statsq", jj - 1)])
                    else:
                        v.wait_ge(cve.sem, M["p1_init"])
                    base = (jj % 2) * NF * C
                    ins = v.tensor_copy(xbf_sb[:], x_sb[:, base:base + NF * C])
                    M[("xbf", jj)] = cve.inc(ins)
                b1.vector(v_xbf)

                def t_stat1(t, jj=jj):
                    t.wait_ge(cve.sem, M[("xbf", jj)])
                    if jj == 0:
                        t.wait_ge(cve.sem, M["wo_scaled"])
                        t.wait_ge(s_ld, 16 * nsetup[0])
                    else:
                        t.wait_ge(cve.sem, M[("xn", jj - 1)])
                    ins = None
                    for f in range(NF):
                        ins = t.matmul(pS[0:1, :], ones_m1[:], fsl(xbf_sb, f),
                                       start=(f == 0), stop=(f == NF - 1))
                    M[("statsum", jj)] = cpe.inc(ins)
                b1.tensor(t_stat1)

                def v_rowA1(v, jj=jj):
                    v.wait_ge(cpe.sem, M[("statsum", jj)])
                    ins = v.tensor_scalar(rows_sb[:, 0:C], pS[0:1, :], 1.0 / D,
                                          None, ALU.mult)
                    M[("rowA1", jj)] = cve.inc(ins)
                b1.vector(v_rowA1)

                def s_sq(s, jj=jj):
                    s.wait_ge(cpe.sem, M[("statsum", jj)])
                    ins = s.activation(xbf_sb[:], xbf_sb[:], AF.Square)
                    M[("sq", jj)] = csc.inc(ins)
                b1.scalar(s_sq)

            def prep_ln_b1(jj):
                def t_stat2(t, jj=jj):
                    t.wait_ge(csc.sem, M[("sq", jj)])
                    t.wait_ge(cve.sem, M[("rowA1", jj)])
                    ins = None
                    for f in range(NF):
                        ins = t.matmul(pS[0:1, :], ones_m1[:], fsl(xbf_sb, f),
                                       start=(f == 0), stop=(f == NF - 1))
                    M[("statsq", jj)] = cpe.inc(ins)
                b1.tensor(t_stat2)

            def prep_ln_b2(jj):
                def v_rowAB(v, jj=jj):
                    v.wait_ge(cpe.sem, M[("statsq", jj)])
                    v.tensor_scalar(rows_sb[:, C:2 * C], pS[0:1, :], 1.0 / D,
                                    None, ALU.mult)
                    v.tensor_mul(rows_sb[:, 2 * C:3 * C], rows_sb[:, 0:C], rows_sb[:, 0:C])
                    ins = v.tensor_sub(rows_sb[:, 2 * C:3 * C], rows_sb[:, C:2 * C],
                                       rows_sb[:, 2 * C:3 * C])
                    M[("var", jj)] = cve.inc(ins)
                b1.vector(v_rowAB)

            def prep_ln_b3(jj):
                def s_rowC(s, jj=jj):
                    s.wait_ge(cve.sem, M[("var", jj)])
                    if jj >= 1:
                        s.wait_ge(s_lnr, 16 * jj)
                    s.activation(rows_sb[:, 3 * C:4 * C], rows_sb[:, 2 * C:3 * C],
                                 AF.Ln, bias=eps_sb[:])
                    ins = s.activation(rows_bf[:, 0:C], rows_sb[:, 3 * C:4 * C],
                                       AF.Exp, scale=-0.5)
                    M[("rstd", jj)] = csc.inc(ins)
                b1.scalar(s_rowC)

            def prep_ln_b4(jj):
                def v_rowD(v, jj=jj):
                    v.wait_ge(csc.sem, M[("rstd", jj)])
                    ins = v.tensor_tensor(rows_bf[:, C:2 * C], rows_sb[:, 0:C],
                                          rows_bf[:, 0:C], ALU.mult)
                    M[("mr", jj)] = cve.inc(ins)
                b1.vector(v_rowD)

            def emit_rows_store(jj):
                def g_rows(g, jj=jj):
                    g.wait_ge(cve.sem, M[("mr", jj)])
                    g.dma_start(out=rows_dram[jj:jj + 1, :],
                                in_=rows_bf[0:1, 0:2 * C]).then_inc(s_lnr, 16)
                b1.gpsimd(g_rows)

            def prep_ln_tail(jj):
                def t_bcR(t, jj=jj):
                    t.wait_ge(cve.sem, M[("mr", jj)])
                    t.wait_ge(cve.sem, M[("var", jj)])
                    ins = t.matmul(pS[:], ones_k1[:], rows_bf[:, 0:C],
                                   start=True, stop=True)
                    M[("bcR", jj)] = cpe.inc(ins)
                b1.tensor(t_bcR)

                def t_bcM(t, jj=jj):
                    if jj >= 1:
                        t.wait_ge(cve.sem, M[("zfree", jj - 1)])
                    ins = t.matmul(pZ[:], ones_k1[:], rows_bf[:, C:2 * C],
                                   start=True, stop=True)
                    M[("bcM", jj)] = cpe.inc(ins)
                b1.tensor(t_bcM)

                def v_norm(v, jj=jj):
                    v.wait_ge(cpe.sem, M[("bcM", jj)])
                    if jj >= 1:
                        v.wait_ge(cpe.sem, M[("qfm", jj - 1, HEADS - 1)])
                    base = (jj % 2) * NF * C
                    x3 = x_sb[:, base:base + NF * C].rearrange("p (f t) -> p f t", t=C)
                    _, r3 = bcast_pair(x3, pS[:, :])
                    v.tensor_tensor(x3, x3, r3, ALU.mult)
                    _, m3 = bcast_pair(x3, pZ[:, :])
                    xn3 = xn_sb[:].rearrange("p (f t) -> p f t", t=C)
                    ins = v.tensor_tensor(xn3, x3, m3, ALU.subtract)
                    M[("xn", jj)] = cve.inc(ins)
                    M[("xfree", jj)] = cve.n
                b1.vector(v_norm)

            emit_load(0)
            emit_load(1)
            prep_ln_a(0)
            prep_ln_b1(0)
            prep_ln_b2(0)
            prep_ln_b3(0)
            prep_ln_b4(0)
            emit_rows_store(0)
            prep_ln_tail(0)

            for j in range(NCH):
                if j + 2 < NCH:
                    emit_load(j + 2)

                for h in range(HEADS):
                    def t_kfm(t, j=j, h=h):
                        pk = sA if h % 2 == 0 else sB
                        if h == 0:
                            t.wait_ge(cve.sem, M[("xn", j)])
                            if j >= 1:
                                t.wait_ge(cve.sem, M[("stev", j - 1, 3, 2)])
                        elif h == 1 and j >= 1:
                            t.wait_ge(cve.sem, M[("stev", j - 1, 3, 3)])
                        elif h >= 2:
                            t.wait_ge(csc.sem, M[("efm", j, h - 2)])
                        ins = None
                        for f in range(NF):
                            ins = t.matmul(pk[:], wk_sb[:, f * 512 + 128 * h: f * 512 + 128 * (h + 1)],
                                           fsl(xn_sb, f), start=(f == 0), stop=(f == NF - 1))
                        M[("kfm", j, h)] = cpe.inc(ins)
                    b1.tensor(t_kfm)

                    def s_efm(s, j=j, h=h):
                        pk = sA if h % 2 == 0 else sB
                        s.wait_ge(cpe.sem, M[("kfm", j, h)])
                        if j >= 1:
                            s.wait_ge(cpe.sem, M[("st", j - 1, h, 3)])
                        ins = s.activation(fsl(efm_sb, h), pk[:], AF.Exp)
                        M[("efm", j, h)] = csc.inc(ins)
                    b1.scalar(s_efm)

                def e_T(h):
                    def f(t, j=j, h=h):
                        t.wait_ge(csc.sem, M[("efm", j, h)])
                        if h >= 1:
                            t.wait_ge(cve.sem, M[("etmev", j, h - 1)])
                        elif j >= 1:
                            t.wait_ge(cve.sem, M[("etmev", j - 1, HEADS - 1)])
                        ins = None
                        for ts in range(4):
                            ins = t.transpose(pT[:, ts * 128:(ts + 1) * 128],
                                              efm_sb[:, h * C + ts * 128: h * C + (ts + 1) * 128],
                                              ident_sb[:])
                        M[("tT", j, h)] = cpe.inc(ins)
                    b1.tensor(f)

                def e_etmev(h):
                    def f(v, j=j, h=h):
                        v.wait_ge(cpe.sem, M[("tT", j, h)])
                        if j >= 1:
                            v.wait_ge(cpe.sem, M[("zc", j - 1, h)])
                        ins = v.tensor_copy(etm_sb[:, h * 512:(h + 1) * 512], pT[:])
                        M[("etmev", j, h)] = cve.inc(ins)
                    b1.vector(f)

                for ts in range(4):
                    def t_vtm(t, j=j, ts=ts):
                        pk = sA if ts % 2 == 0 else sB
                        if ts == 0:
                            t.wait_ge(csc.sem, M[("efm", j, 2)])
                        elif ts == 1:
                            t.wait_ge(csc.sem, M[("efm", j, 3)])
                        else:
                            t.wait_ge(cve.sem, M[("vev", j, ts - 2)])
                        ins = None
                        for f in range(NF):
                            ins = t.matmul(pk[:], xn_sb[:, f * C + 128 * ts: f * C + 128 * (ts + 1)],
                                           wv_sb[:, f * 512:(f + 1) * 512],
                                           start=(f == 0), stop=(f == NF - 1))
                        M[("vtm", j, ts)] = cpe.inc(ins)
                    b1.tensor(t_vtm)

                    def v_vev(v, j=j, ts=ts):
                        pk = sA if ts % 2 == 0 else sB
                        v.wait_ge(cpe.sem, M[("vtm", j, ts)])
                        if j >= 1:
                            v.wait_ge(cpe.sem, M[("omm", j - 1, HEADS - 1)])
                        ins = v.tensor_copy(vtm_sb[:, ts * 512:(ts + 1) * 512], pk[:])
                        M[("vev", j, ts)] = cve.inc(ins)
                    b1.vector(v_vev)

                for h in range(HEADS):
                    def t_qfm(t, j=j, h=h):
                        pk = sA if h % 2 == 0 else sB
                        if h == 0:
                            t.wait_ge(cve.sem, M[("vev", j, 2)])
                        elif h == 1:
                            t.wait_ge(cve.sem, M[("vev", j, 3)])
                        else:
                            t.wait_ge(csc.sem, M[("sqh", j, h - 2)])
                        ins = None
                        for f in range(NF):
                            ins = t.matmul(pk[:], wq_sb[:, f * 512 + 128 * h: f * 512 + 128 * (h + 1)],
                                           fsl(xn_sb, f), start=(f == 0), stop=(f == NF - 1))
                        M[("qfm", j, h)] = cpe.inc(ins)
                    b1.tensor(t_qfm)

                    def s_silu(s, j=j, h=h):
                        pk = sA if h % 2 == 0 else sB
                        s.wait_ge(cpe.sem, M[("qfm", j, h)])
                        if j >= 1:
                            s.wait_ge(cve.sem, M[("qh", j - 1, h)])
                        ins = s.activation(fsl(sq4_sb, h), pk[:], AF.Silu)
                        M[("sqh", j, h)] = csc.inc(ins)
                    b1.scalar(s_silu)

                def e_au(h):
                    def f(t, j=j, h=h):
                        t.wait_ge(cve.sem, M[("etmev", j, h)])
                        t.wait_ge(cve.sem, M[("vev", j, 3)])
                        if j >= 1:
                            t.wait_ge(cve.sem, M[("abf", j - 1, h)])
                        ins = None
                        for ts in range(4):
                            ins = t.matmul(pA[:, h * 128:(h + 1) * 128],
                                           eslot(h, ts),
                                           vtm_sb[:, ts * 512 + 128 * h: ts * 512 + 128 * (h + 1)],
                                           start=(j == 0 and ts == 0),
                                           stop=(j == NCH - 1 and ts == 3))
                        M[("aupd", j, h)] = cpe.inc(ins)
                    b1.tensor(f)

                if j + 1 < NCH:
                    prep_ln_a(j + 1)

                def t_zc(h):
                    def f(t, j=j, h=h):
                        pz = pZ if h % 2 == 0 else oB
                        t.wait_ge(cve.sem, M[("etmev", j, h)])
                        if h == 0:
                            t.wait_ge(cve.sem, M[("xn", j)])
                        elif h == 1:
                            if j >= 1:
                                t.wait_ge(cve.sem, M[("zfv", j - 1, 3)])
                        else:
                            t.wait_ge(cve.sem, M[("zfv", j, h - 2)])
                        ins = None
                        for ts in range(4):
                            ins = t.matmul(pz[:], eslot(h, ts), u_sb[:, ts * C:(ts + 1) * C],
                                           start=(ts == 0), stop=(ts == 3))
                        M[("zc", j, h)] = cpe.inc(ins)
                    b1.tensor(f)

                def v_zf(h):
                    def f(v, j=j, h=h):
                        pz = pZ if h % 2 == 0 else oB
                        par = h % 2
                        v.wait_ge(cpe.sem, M[("zc", j, h)])
                        if h >= 2:
                            v.wait_ge(csc.sem, M[("rcpln", j, 0)])
                        ins = v.tensor_scalar(zf_sb[:, par * C:(par + 1) * C], pz[:],
                                              zcar_sb[:, h:h + 1], None, ALU.add)
                        M[("zfv", j, h)] = cve.inc(ins)
                        if h == 2:
                            M[("zfree", j)] = M[("zfv", j, h)]
                        ins = v.tensor_copy(zcar_sb[:, h:h + 1],
                                            zf_sb[:, par * C + C - 1: par * C + C])
                        cve.inc(ins)
                    b1.vector(f)

                def s_rcp(p):
                    def f(s, j=j, p=p):
                        s.wait_ge(cve.sem, M[("zfv", j, 2 * p + 1)])
                        if p == 1:
                            s.wait_ge(cve.sem, M[("qh", j, 1)])
                        elif j >= 1:
                            s.wait_ge(cve.sem, M[("qh", j - 1, 3)])
                        ins = s.activation(rr_sb[:], zf_sb[:], AF.Ln)
                        M[("rcpln", j, p)] = csc.inc(ins)
                        ins = s.activation(rr_sb[:], rr_sb[:], AF.Exp, scale=-1.0,
                                           bias=lnso_sb[:])
                        M[("rcpp", j, p)] = csc.inc(ins)
                    b1.scalar(f)

                def v_qh(h):
                    def f(v, j=j, h=h):
                        par = h % 2
                        v.wait_ge(csc.sem, M[("sqh", j, h)])
                        v.wait_ge(csc.sem, M[("rcpp", j, h // 2)])
                        if j >= 1:
                            v.wait_ge(cpe.sem, M[("omm", j - 1, h)])
                        ins = v.tensor_tensor(fsl(qh_sb, h), fsl(sq4_sb, h),
                                              rr_sb[:, par * C:(par + 1) * C], ALU.mult)
                        M[("qh", j, h)] = cve.inc(ins)
                    b1.vector(f)

                def t_st(h, r):
                    def f(t, j=j, h=h, r=r):
                        pk = sA if r % 2 == 0 else sB
                        t.wait_ge(cve.sem, M[("qh", j, h)])
                        if r >= 2:
                            t.wait_ge(cve.sem, M[("stev", j, h, r - 2)])
                        elif h >= 1:
                            t.wait_ge(cve.sem, M[("stev", j, h - 1, r + 2)])
                        elif r == 0:
                            t.wait_ge(csc.sem, M[("sqh", j, 2)])
                        else:
                            t.wait_ge(csc.sem, M[("sqh", j, 3)])
                        ins = t.matmul(pk[:], efm_sb[:, h * C + 128 * r: h * C + 128 * (r + 1)],
                                       fsl(qh_sb, h), start=True, stop=True)
                        M[("st", j, h, r)] = cpe.inc(ins)
                    b1.tensor(f)

                def v_stev(h, r):
                    def f(v, j=j, h=h, r=r):
                        pk = sA if r % 2 == 0 else sB
                        v.wait_ge(cpe.sem, M[("st", j, h, r)])
                        ins = v.tensor_tensor(ssb_sb[:, r * C:(r + 1) * C], pk[:],
                                              u_sb[:, r * C:(r + 1) * C], ALU.mult)
                        M[("stev", j, h, r)] = cve.inc(ins)
                    b1.vector(f)

                def t_o(h):
                    def f(t, j=j, h=h):
                        if h >= 1:
                            t.wait_ge(cve.sem, M[("oev", j, h - 1)])
                        elif j >= 1:
                            t.wait_ge(cve.sem, M[("oev", j - 1, HEADS - 1)])
                        t.wait_ge(cve.sem, M[("stev", j, h, 0)])
                        ins = t.matmul(oA[:], abf_sb[:, h * 128:(h + 1) * 128],
                                       fsl(qh_sb, h), start=True, stop=False)
                        for ts in range(4):
                            if ts >= 1:
                                t.wait_ge(cve.sem, M[("stev", j, h, ts)])
                            ins = t.matmul(oA[:, 128 * ts:C],
                                           vtm_sb[:, ts * 512 + 128 * h: ts * 512 + 128 * (h + 1)],
                                           ssb_sb[:, ts * C + 128 * ts:(ts + 1) * C],
                                           start=False, stop=(ts == 3))
                        M[("omm", j, h)] = cpe.inc(ins)
                    b1.tensor(f)

                def v_oev(h):
                    def f(v, j=j, h=h):
                        v.wait_ge(cpe.sem, M[("omm", j, h)])
                        if o_store_cnt[0] >= 2:
                            v.wait_ge(s_o, 16 * (o_store_cnt[0] - 1))
                        ins = v.tensor_copy(osb_sb[:, (h % 2) * 512:(h % 2 + 1) * 512],
                                            oA[:])
                        M[("oev", j, h)] = cve.inc(ins)
                    b1.vector(f)

                def g_ost(h):
                    def f(g, j=j, h=h):
                        g.wait_ge(cve.sem, M[("oev", j, h)])
                        if j < JSPLIT:
                            od, r0 = o0_dram, (j * HEADS + h) * 128
                        else:
                            od, r0 = o1_dram, ((j - JSPLIT) * HEADS + h) * 128
                        g.dma_start(out=od[r0:r0 + 128, :],
                                    in_=osb_sb[:, (h % 2) * 512:(h % 2 + 1) * 512]
                                    ).then_inc(s_o, 16)
                        o_store_cnt[0] += 1
                    b1.gpsimd(f)

                def v_abf(h):
                    def f(v, j=j, h=h):
                        v.wait_ge(cpe.sem, M[("omm", j, h)])
                        v.wait_ge(cpe.sem, M[("aupd", j, h)])
                        ins = v.tensor_copy(abf_sb[:, h * 128:(h + 1) * 128],
                                            pA[:, h * 128:(h + 1) * 128])
                        M[("abf", j, h)] = cve.inc(ins)
                    b1.vector(f)

                e_T(0); e_etmev(0); t_zc(0); e_au(0)
                e_T(1); e_etmev(1); t_zc(1); e_au(1)
                v_zf(0); v_zf(1)
                s_rcp(0)
                e_T(2); e_etmev(2); t_zc(2); e_au(2)
                v_qh(0)
                for r in range(4):
                    t_st(0, r); v_stev(0, r)
                v_zf(2)
                e_T(3); e_etmev(3); t_zc(3); e_au(3)
                v_qh(1)
                t_o(0); v_oev(0); g_ost(0); v_abf(0)
                for r in range(4):
                    t_st(1, r); v_stev(1, r)
                v_zf(3)
                s_rcp(1)
                if j + 1 < NCH:
                    prep_ln_b1(j + 1)
                v_qh(2)
                if j + 1 < NCH:
                    prep_ln_b2(j + 1)
                t_o(1); v_oev(1); g_ost(1); v_abf(1)
                for r in range(4):
                    t_st(2, r); v_stev(2, r)
                v_qh(3)
                t_o(2); v_oev(2); g_ost(2); v_abf(2)
                for r in range(4):
                    t_st(3, r); v_stev(3, r)
                if j + 1 < NCH:
                    prep_ln_b3(j + 1)
                t_o(3); v_oev(3); g_ost(3); v_abf(3)
                if j + 1 < NCH:
                    prep_ln_b4(j + 1)

                if j + 1 < NCH:
                    prep_ln_tail(j + 1)
                    emit_rows_store(j + 1)

                if j == JSPLIT - 1:
                    def g_cc0(g):
                        g.wait_ge(s_o, 16 * JSPLIT * HEADS)
                        g.collective_compute(
                            "AllGather", ALU.bypass, replica_groups=groups,
                            ins=[o0_dram.ap()], outs=[cc0.ap()],
                        ).then_inc(s_cc, 1)
                    b1.gpsimd(g_cc0)

                if j < 2 * NHC:
                    def g_cast(g, j=j):
                        par = j % 2
                        src = wgTd if j < NHC else wuTd
                        dst = wgbf if j < NHC else wubf
                        hp = j % NHC
                        if j >= 2:
                            g.wait_ge(s_pcw, 16 * (j - 1))
                        g.dma_start(
                            out=cast_sb[:, par * NF * HC:(par + 1) * NF * HC]
                            .rearrange("p (f h) -> p f h", h=HC),
                            in_=src.rearrange("(f p) h -> p f h", p=128)[:, :, hp * HC:(hp + 1) * HC],
                        ).then_inc(s_pcr, 16)
                        g.wait_ge(s_pcr, 16 * (j + 1))
                        g.dma_start(
                            out=dst[hp * 128:(hp + 1) * 128, :],
                            in_=cast_sb[:, par * NF * HC:(par + 1) * NF * HC],
                        ).then_inc(s_pcw, 16)
                    b1.gpsimd(g_cast)

            def g_cc(g):
                g.wait_ge(s_o, 16 * NCH * HEADS)
                g.collective_compute(
                    "AllGather", ALU.bypass, replica_groups=groups,
                    ins=[o1_dram.ap()], outs=[cc1.ap()],
                ).then_inc(s_cc, 1)
            b1.gpsimd(g_cc)

            def s_fin1(s):
                s.wait_ge(cpe.sem, cpe.n)
                s.wait_ge(s_x, 16 * NCH)
                s.wait_ge(s_pcw, 16 * 2 * NHC)
                s.wait_ge(s_lnr, 16 * NCH)
                s.copy(rows_sb[0:1, 0:1], rows_sb[0:1, 0:1]).then_inc(s_p1, 1)
            b1.scalar(s_fin1)

            def v_fin1(v):
                v.engine_nop().then_inc(s_p1, 1)
            b1.vector(v_fin1)

            def g_fin1(g):
                g.engine_nop().then_inc(s_p1, 1)
            b1.gpsimd(g_fin1)

        es2 = contextlib.ExitStack()
        with es2:
            sb2 = lambda *a: es2.enter_context(nc.sbuf_tensor(*a))
            ps2 = lambda *a: es2.enter_context(nc.psum_tensor(*a))
            x2_sb = sb2("x2_sb", [128, 2 * NF * TS], F32)
            xt_sb = sb2("xt_sb", [128, 2 * TS], F32)
            xn2_sb = sb2("xn2_sb", [128, NF * TS], BF16)
            ob2_sb = sb2("ob2_sb", [128, 2 * NF * 512], BF16)
            wgs_sb = sb2("wgs_sb", [128, 2 * NF * HC], BF16)
            wus_sb = sb2("wus_sb", [128, 2 * NF * HC], BF16)
            h_sb = sb2("h_sb", [128, (HID // 128) * TS], BF16)
            g1_sb = sb2("g1_sb", [128, TS], BF16)
            th_sb = sb2("th_sb", [128, TS], F32)
            gsig_sb = sb2("gsig_sb", [128, TS], BF16)
            osq_sb = sb2("osq_sb", [128, 4 * 512], BF16)
            gs_sb = sb2("gs_sb", [128, 2 * TS], F32)
            o2_sb = sb2("o2_sb", [128, 2 * 512], F32)
            lnr2_sb = sb2("lnr2_sb", [128, 2 * 2 * C], BF16)
            qB1 = ps2("qB1", [128, TS], F32)
            qB2 = ps2("qB2", [128, TS], F32)
            qG1 = ps2("qG1", [128, TS], F32)
            qG2 = ps2("qG2", [128, TS], F32)
            qG3 = ps2("qG3", [128, TS], F32)
            qG4 = ps2("qG4", [128, TS], F32)
            qO1 = ps2("qO1", [128, TS], F32)
            qO2 = ps2("qO2", [128, TS], F32)
            b2 = es2.enter_context(nc.Block())

            def f2(buf, f, w=TS):
                return buf[:, f * w:(f + 1) * w]

            def xv2(t, f):
                base = (t % 2) * NF * TS
                return x2_sb[:, base + f * TS: base + (f + 1) * TS]

            st_cnt = [0]

            def emit_stream(k, wait_key=None):
                tp, hp = divmod(k, NHC)
                def q_wg(q, hp=hp, par=k % 2, wait_key=wait_key):
                    if wait_key is not None:
                        q.wait_ge(cpe.sem, M[wait_key])
                    q.dma_start(out=wgs_sb[:, par * NF * HC:(par + 1) * NF * HC],
                                in_=wgbf[hp * 128:(hp + 1) * 128, :]).then_inc(s_wg, 16)
                b2.sync(q_wg)
                def q_wu(q, hp=hp, par=k % 2, wait_key=wait_key):
                    if wait_key is not None:
                        q.wait_ge(cpe.sem, M[wait_key])
                    q.dma_start(out=wus_sb[:, par * NF * HC:(par + 1) * NF * HC],
                                in_=wubf[hp * 128:(hp + 1) * 128, :]).then_inc(s_wu, 16)
                b2.gpsimd(q_wu)

            def barrier(e):
                e.wait_ge(s_p1, 3)
            for sec in (b2.sync, b2.tensor, b2.scalar, b2.vector, b2.gpsimd):
                sec(barrier)
            def emit_lnr(t):
                def g_lnr(g, t=t):
                    if t >= 2:
                        g.wait_ge(cve.sem, M[("xn2", t - 2)])
                    g.indirect_dma_start(
                        lnr2_sb[:, (t % 2) * 1024:(t % 2 + 1) * 1024],
                        None,
                        rows_dram[:, :], IndirectOffsetOnAxis(ridx_sb[:, t:t + 1], 0),
                    ).then_inc(s_lr, 16)
                b2.gpsimd(g_lnr)

            def norm2(t):
                def v_norm(v, t=t):
                    par = t % 2
                    v.wait_ge(s_lr, 16 * (t + 1))
                    v.wait_ge(s_x2, 16 * (t + 1))
                    if t >= 1:
                        v.wait_ge(cpe.sem, M[("ummx", t - 1, NHC - 1, 3)])
                        v.wait_ge(csc.sem, M[("rstd2r", t - 1)])
                    rsl = lnr2_sb[:, par * 1024: par * 1024 + 512]
                    msl = lnr2_sb[:, par * 1024 + 512:(par + 1) * 1024]
                    r1 = rsl.rearrange("p (a t) -> p a t", a=1)
                    m1 = msl.rearrange("p (a t) -> p a t", a=1)
                    ins = None
                    for fp in range(NF // 2):
                        x3 = x2_sb[:, (par * NF + 2 * fp) * TS:(par * NF + 2 * fp + 2) * TS] \
                            .rearrange("p (f t) -> p f t", t=TS)
                        xt3 = xt_sb[:].rearrange("p (f t) -> p f t", t=TS)
                        _, r3 = bass.broadcast_tensor_aps(x3, r1)
                        v.tensor_tensor(xt3, x3, r3, ALU.mult)
                        xn3 = xn2_sb[:, 2 * fp * TS:(2 * fp + 2) * TS] \
                            .rearrange("p (f t) -> p f t", t=TS)
                        _, m3 = bass.broadcast_tensor_aps(x3, m1)
                        ins = v.tensor_tensor(xn3, xt3, m3, ALU.subtract)
                    M[("xn2", t)] = cve.inc(ins)
                b2.vector(v_norm)

            def emit_load2(t):
                def s_load2(s, t_i=t):
                    if t_i >= 2:
                        s.wait_ge(cve.sem, M[("fadd", t_i - 2, NF - 1)])
                    s.dma_start(
                        out=x2_sb[:, (t_i % 2) * NF * TS:((t_i % 2) + 1) * NF * TS]
                        .rearrange("p (f t) -> p f t", t=TS),
                        in_=xoT.rearrange("(f p) t -> p f t", p=128)[:, :, t_i * TS:(t_i + 1) * TS],
                    ).then_inc(s_x2, 16)
                b2.sync(s_load2)

            emit_load2(0)
            emit_lnr(0)
            emit_stream(0)
            emit_stream(1)
            norm2(0)

            for t_i in range(NTS):
                if t_i >= 1:
                    emit_load2(t_i)

                if t_i == 0:
                    emit_lnr(1)
                elif t_i + 1 < NTS:
                    emit_lnr(t_i + 1)

                def g_og(g, t_i=t_i):
                    if t_i == 0:
                        g.wait_ge(s_cc, 1)
                    if t_i == JSPLIT // 2:
                        g.wait_ge(s_cc, 2)
                    if t_i >= 2:
                        g.wait_ge(cpe.sem, M[("dw", t_i - 2, NF - 1)])
                    ccs = cc0 if t_i < JSPLIT // 2 else cc1
                    for H in range(8):
                        col = t_i * 8 + H
                        g.indirect_dma_start(
                            ob2_sb[:, ((t_i % 2) * NF + H) * 512:((t_i % 2) * NF + H + 1) * 512],
                            None,
                            ccs[:, :], IndirectOffsetOnAxis(oidx_sb[:, col:col + 1], 0),
                        ).then_inc(s_og, 16)
                b2.gpsimd(g_og)

                for hc in range(NHC):
                    for hf in range(4):
                        def t_g(t, t_i=t_i, hc=hc, hf=hf):
                            k = t_i * NHC + hc
                            par = k % 2
                            pg = qG1 if hf % 2 == 0 else qG2
                            t.wait_ge(cve.sem, M[("xn2", t_i)])
                            t.wait_ge(s_wg, 16 * (k + 1))
                            if (t_i, hc, hf) > (0, 0, 1):
                                prev = (t_i, hc, hf - 2) if hf >= 2 else \
                                       ((t_i, hc - 1, hf + 2) if hc >= 1 else (t_i - 1, NHC - 1, hf + 2))
                                t.wait_ge(csc.sem, M[("silu",) + prev])
                            ins = None
                            for f in range(NF):
                                ins = t.matmul(
                                    pg[:],
                                    wgs_sb[:, par * NF * HC + f * HC + 128 * hf:
                                           par * NF * HC + f * HC + 128 * (hf + 1)],
                                    f2(xn2_sb, f), start=(f == 0), stop=(f == NF - 1))
                            M[("gmm", t_i, hc, hf)] = cpe.inc(ins)
                        b2.tensor(t_g)

                        def s_silu(s, t_i=t_i, hc=hc, hf=hf):
                            pg = qG1 if hf % 2 == 0 else qG2
                            s.wait_ge(cpe.sem, M[("gmm", t_i, hc, hf)])
                            kt = hc * 4 + hf
                            kprev = t_i * 24 + kt - 2
                            if kprev >= 0:
                                tp, kp = divmod(kprev, 24)
                                s.wait_ge(cve.sem, M[("hst", tp, kp // 4, kp % 4)])
                            par = kt % 2
                            ins = s.activation(gs_sb[:, par * TS:(par + 1) * TS], pg[:],
                                               AF.Silu, scale=Q127)
                            M[("silu", t_i, hc, hf)] = csc.inc(ins)
                        b2.scalar(s_silu)

                        def t_u(t, t_i=t_i, hc=hc, hf=hf):
                            k = t_i * NHC + hc
                            par = k % 2
                            pu = qG3 if hf % 2 == 0 else qG4
                            t.wait_ge(s_wu, 16 * (k + 1))
                            if (t_i, hc, hf) > (0, 0, 1):
                                prev = (t_i, hc, hf - 2) if hf >= 2 else \
                                       ((t_i, hc - 1, hf + 2) if hc >= 1 else (t_i - 1, NHC - 1, hf + 2))
                                t.wait_ge(cve.sem, M[("hst",) + prev])
                            ins = None
                            for f in range(NF):
                                ins = t.matmul(
                                    pu[:],
                                    wus_sb[:, par * NF * HC + f * HC + 128 * hf:
                                           par * NF * HC + f * HC + 128 * (hf + 1)],
                                    f2(xn2_sb, f), start=(f == 0), stop=(f == NF - 1))
                            M[("umm", t_i, hc)] = M[("ummx", t_i, hc, hf)] = cpe.inc(ins)
                        b2.tensor(t_u)

                        def v_h(v, t_i=t_i, hc=hc, hf=hf):
                            pu = qG3 if hf % 2 == 0 else qG4
                            v.wait_ge(cpe.sem, M[("ummx", t_i, hc, hf)])
                            v.wait_ge(csc.sem, M[("silu", t_i, hc, hf)])
                            kt = hc * 4 + hf
                            ins = v.scalar_tensor_tensor(h_sb[:, kt * TS:(kt + 1) * TS],
                                                         pu[:], Q127SQ,
                                                         gs_sb[:, (kt % 2) * TS:(kt % 2 + 1) * TS],
                                                         ALU.mult, ALU.mult)
                            M[("hst", t_i, hc, hf)] = cve.inc(ins)
                        b2.vector(v_h)
                        if hf == 3:
                            knext = (t_i * NHC + hc) + 2
                            if knext < NTS * NHC:
                                emit_stream(knext, wait_key=("umm", t_i, hc))

                    if hc == 0:
                        def t_g1(t, t_i=t_i):
                            t.wait_ge(cve.sem, M[("xn2", t_i)])
                            if t_i >= 1:
                                t.wait_ge(cve.sem, M[("gsig", t_i - 1, 7)])
                            ins = None
                            for f in range(NF):
                                ins = t.matmul(qO2[:, 0:TS], wg1_sb[:, f * 128:(f + 1) * 128],
                                               f2(xn2_sb, f), start=(f == 0), stop=(f == NF - 1))
                            M[("g1", t_i)] = cpe.inc(ins)
                        b2.tensor(t_g1)

                        def s_g1(s, t_i=t_i):
                            s.wait_ge(cpe.sem, M[("g1", t_i)])
                            ins = s.copy(g1_sb[:], qO2[:, 0:TS])
                            M[("g1c", t_i)] = csc.inc(ins)
                        b2.scalar(s_g1)

                    if hc in (1, 2):
                        half = hc - 1
                        def v_osq(v, t_i=t_i, half=half):
                            if half == 0:
                                v.wait_ge(s_og, 16 * 8 * (t_i + 1))
                                if t_i >= 1:
                                    v.wait_ge(cpe.sem, M[("osmm", t_i - 1, 7)])
                            else:
                                v.wait_ge(cpe.sem, M[("osmm", t_i, 3)])
                            base = (t_i % 2) * NF * 512 + half * 4 * 512
                            ins = v.tensor_mul(osq_sb[:],
                                               ob2_sb[:, base: base + 4 * 512],
                                               ob2_sb[:, base: base + 4 * 512])
                            M[("osq1", t_i, half)] = cve.inc(ins)
                        b2.vector(v_osq)

                        for gf in range(4 * half, 4 * half + 4):
                            def t_osmm(t, t_i=t_i, gf=gf, half=half):
                                if gf % 4 == 0:
                                    t.wait_ge(cve.sem, M[("osq1", t_i, half)])
                                    if gf == 0 and t_i >= 1:
                                        t.wait_ge(cve.sem, M[("fadd", t_i - 1, 6)])
                                ins = t.matmul(qO1[0:1, 0:TS], ones_m1[:],
                                               osq_sb[:, (gf % 4) * 512:(gf % 4 + 1) * 512],
                                               start=(gf == 0), stop=(gf == 7))
                                M[("osmm", t_i, gf)] = cpe.inc(ins)
                            b2.tensor(t_osmm)

                    if hc == 2:
                        def s_rmsrow(s, t_i=t_i):
                            s.wait_ge(cpe.sem, M[("osmm", t_i, 7)])
                            s.activation(xt_sb[0:1, 0:TS], qO1[0:1, 0:TS],
                                         AF.Ln, bias=eps_sb[:], scale=1.0 / D)
                            ins = s.activation(rows_bf[:, C:C + TS], xt_sb[0:1, 0:TS],
                                               AF.Exp, scale=-0.5)
                            M[("rstd2r", t_i)] = csc.inc(ins)
                        b2.scalar(s_rmsrow)

                        def t_rmsbc(t, t_i=t_i):
                            t.wait_ge(csc.sem, M[("rstd2r", t_i)])
                            ins = t.matmul(qO1[:], ones_k1[:], rows_bf[:, C:C + TS],
                                           start=True, stop=True)
                            M[("rmsbc", t_i)] = cpe.inc(ins)
                        b2.tensor(t_rmsbc)

                    if hc >= 2 and hc <= 4:
                        for gf in ((0, 1, 2), (3, 4, 5), (6, 7))[hc - 2]:
                            def t_gate(t, t_i=t_i, gf=gf):
                                pg = qB2 if gf % 2 == 0 else qO2
                                if gf <= 1:
                                    t.wait_ge(csc.sem, M[("g1c", t_i)])
                                if gf == 0 and t_i >= 1:
                                    t.wait_ge(cve.sem, M[("fadd", t_i - 1, NF - 1)])
                                if gf >= 2:
                                    t.wait_ge(cve.sem, M[("gsig", t_i, gf - 2)])
                                ins = t.matmul(pg[:, 0:TS], wg2_sb[:, gf * 128:(gf + 1) * 128],
                                               g1_sb[:], start=True, stop=True)
                                M[("gate", t_i, gf)] = cpe.inc(ins)
                            b2.tensor(t_gate)

                            def s_tanh(s, t_i=t_i, gf=gf):
                                pg = qB2 if gf % 2 == 0 else qO2
                                s.wait_ge(cpe.sem, M[("gate", t_i, gf)])
                                if gf >= 1:
                                    s.wait_ge(cve.sem, M[("gsig", t_i, gf - 1)])
                                ins = s.activation(th_sb[:], pg[:, 0:TS], AF.Tanh, scale=0.5)
                                M[("tanh", t_i, gf)] = csc.inc(ins)
                            b2.scalar(s_tanh)

                            def v_gsig(v, t_i=t_i, gf=gf):
                                pg = qB2 if gf % 2 == 0 else qO2
                                v.wait_ge(csc.sem, M[("tanh", t_i, gf)])
                                ins = v.scalar_tensor_tensor(gsig_sb[:], th_sb[:],
                                                             1.0, pg[:, 0:TS], ALU.add, ALU.mult)
                                M[("gsig", t_i, gf)] = cve.inc(ins)
                                if gf == 0:
                                    v.wait_ge(cpe.sem, M[("rmsbc", t_i)])
                                base = (t_i % 2) * NF * 512
                                sl = ob2_sb[:, base + gf * 512: base + (gf + 1) * 512]
                                v.tensor_tensor(sl, sl, qO1[:], ALU.mult)
                                ins = v.tensor_tensor(sl, sl, gsig_sb[:], ALU.mult)
                                M[("ogt", t_i, gf)] = cve.inc(ins)
                            b2.vector(v_gsig)

                for dt in range(NF):
                    if dt == 2 and t_i + 1 < NTS:
                        norm2(t_i + 1)
                    def t_dw(t, t_i=t_i, dt=dt):
                        po = qO1 if dt % 2 == 0 else qB2
                        if dt == 0:
                            t.wait_ge(cve.sem, M[("ogt", t_i, 7)])
                            t.wait_ge(cve.sem, M[("hst", t_i, NHC - 1, 3)])
                        if dt == 1:
                            t.wait_ge(cve.sem, M[("gsig", t_i, 6)])
                        if dt >= 2:
                            t.wait_ge(cve.sem, M[("fadd", t_i, dt - 2)])
                        ins = None
                        nk = HID // 128
                        base = (t_i % 2) * NF * 512
                        for kt in range(nk):
                            ins = t.matmul(po[:, 0:TS],
                                           wd_sb[:, kt * D + 128 * dt: kt * D + 128 * (dt + 1)],
                                           h_sb[:, kt * TS:(kt + 1) * TS],
                                           start=(kt == 0), stop=False)
                        for of in range(NF):
                            ins = t.matmul(po[:, 0:TS],
                                           wo_sb[:, of * D + 128 * dt: of * D + 128 * (dt + 1)],
                                           ob2_sb[:, base + of * 512: base + (of + 1) * 512],
                                           start=False, stop=(of == NF - 1))
                        M[("dw", t_i, dt)] = cpe.inc(ins)
                    b2.tensor(t_dw)

                    def v_fadd(v, t_i=t_i, dt=dt):
                        po = qO1 if dt % 2 == 0 else qB2
                        v.wait_ge(cpe.sem, M[("dw", t_i, dt)])
                        if st_cnt[0] >= 2:
                            v.wait_ge(s_st, 16 * (st_cnt[0] - 1))
                        ins = v.tensor_tensor(o2_sb[:, (dt % 2) * 512:(dt % 2) * 512 + TS],
                                              xv2(t_i, dt), po[:, 0:TS], ALU.add)
                        M[("fadd", t_i, dt)] = cve.inc(ins)
                    b2.vector(v_fadd)

                    def s_store(s, t_i=t_i, dt=dt):
                        s.wait_ge(cve.sem, M[("fadd", t_i, dt)])
                        s.dma_start(
                            out=out.rearrange("(f p) t -> p f t", p=128)[:, dt:dt + 1,
                                                                         t_i * TS:(t_i + 1) * TS],
                            in_=o2_sb[:, (dt % 2) * 512:(dt % 2) * 512 + TS],
                        ).then_inc(s_st, 16)
                        st_cnt[0] += 1
                    b2.scalar(s_store)

            def s_fin(s):
                s.wait_ge(s_st, 16 * NTS * NF)
            b2.scalar(s_fin)

    return nc



def _prep_core_inputs(c, x, Wq, Wk, Wv, Wg1, Wg2, g_norm, Wo, wg, wu, wd, T_full, T_own):
    NCH = T_full // C
    NTS = T_own // TS
    b, hg = c // 2, c % 2
    own = [hg + 2 * t for t in range(NTS)]
    NROW0 = JSPLIT * HEADS * 128
    NROW1 = (NCH - JSPLIT) * HEADS * 128
    cT = np.ascontiguousarray
    U = np.triu(np.ones((C, C), dtype=np.float32))
    oidx = np.zeros((128, NTS * 8), dtype=np.int32)
    for t_i in range(NTS):
        j = own[t_i]
        if j < JSPLIT:
            jh, nrow = j, NROW0
        else:
            jh, nrow = j - JSPLIT, NROW1
        for H in range(8):
            slot, lh = H // 4, H % 4
            row0 = slot * nrow + (jh * HEADS + lh) * 128
            oidx[:, t_i * 8 + H] = row0 + np.arange(128)
    ridx = np.tile(np.array(own, dtype=np.int32), (128, 1))
    xo = x[b].reshape(NCH, C, -1)[own].reshape(T_own, -1)
    return {
        "xT": cT(x[b].T),
        "xoT": cT(xo.T),
        "wqT": cT(Wq.T[:, hg * 512:(hg + 1) * 512]),
        "wkT": cT(Wk.T[:, hg * 512:(hg + 1) * 512]),
        "wvT": cT(Wv.T[:, hg * 512:(hg + 1) * 512]),
        "woT": cT(Wo.T),
        "wg1T": cT(Wg1.T),
        "wg2T": cT(Wg2.T),
        "gnorm": cT(g_norm.reshape(NF, 128).T),
        "wgTd": cT(wg.T),
        "wuTd": cT(wu.T),
        "wdTd": cT(wd.T),
        "umask": U,
        "ident": np.eye(128, dtype=np.float32),
        "oidx": oidx,
        "ridx": ridx,
    }


def kernel(L0_attn_hidden_states, L0_attn_Wq, L0_attn_Wk, L0_attn_Wv,
           L0_attn_Wg1, L0_attn_Wg2, L0_attn_g_norm_weight, L0_attn_Wo,
           L0_ffn_w_gate, L0_ffn_w_up, L0_ffn_w_down):
    x = np.asarray(L0_attn_hidden_states, dtype=np.float32)
    B, T, _ = x.shape
    T_own = T // 2
    args = (np.asarray(L0_attn_Wq, np.float32), np.asarray(L0_attn_Wk, np.float32),
            np.asarray(L0_attn_Wv, np.float32), np.asarray(L0_attn_Wg1, np.float32),
            np.asarray(L0_attn_Wg2, np.float32), np.asarray(L0_attn_g_norm_weight, np.float32),
            np.asarray(L0_attn_Wo, np.float32), np.asarray(L0_ffn_w_gate, np.int32),
            np.asarray(L0_ffn_w_up, np.int32), np.asarray(L0_ffn_w_down, np.int32))
    nc = build_program(T, T_own)
    in_maps = [_prep_core_inputs(c, x, *args, T, T_own) for c in range(8)]
    res = run_bass_kernel_spmd(nc, in_maps, core_ids=list(range(8)))
    outp = np.empty((B, T, D), dtype=np.float32)
    for c in range(8):
        b, hg = c // 2, c % 2
        for t in range(T_own // C):
            j = hg + 2 * t
            outp[b, j * C:(j + 1) * C, :] = res.results[c]["out"][:, t * C:(t + 1) * C].T
    return outp
